# version 06
# baseline (speedup 1.0000x reference)
import math
import numpy as np
import ml_dtypes

import concourse.bass as bass
import concourse.mybir as mybir
from concourse.bass_utils import run_bass_kernel_spmd

F32 = mybir.dt.float32
BF16 = mybir.dt.bfloat16
U8 = mybir.dt.uint8
AF = mybir.ActivationFunctionType
ALU = mybir.AluOpType
AXX = mybir.AxisListType.X

B, S, H = 2, 4096, 4096
NH, QL, KVL, NOPE, ROPE, VH = 32, 1536, 512, 128, 64, 128
QH = NOPE + ROPE
TM = 1024
NP = 16
ISCALE = 1.0 / math.sqrt(QH)
FL_TH = [3, 3, 3, 6, 6, 6, 8, 8]
O_COL = [(0, 0), (0, 129), (0, 258), (0, 512), (0, 641), (0, 770),
         (1, 0), (1, 129)]
KV_REG4 = [(0, 0), (0, 512), (1, 0), (1, 512)]


class _Alloc:

    def __init__(self, nc):
        self.nc = nc
        self.scopes = []

    def open(self):
        self.scopes.append([])

    def close(self):
        for g in reversed(self.scopes.pop()):
            g.__exit__(None, None, None)

    def sb(self, name, shape, dt):
        g = self.nc.sbuf_tensor(name, list(shape), dt)
        t = g.__enter__()
        self.scopes[-1].append(g)
        return t


def build_nc():
    nc = bass.Bass("TRN2", target_bir_lowering=False, debug=False)

    d = lambda n, sh, dt: nc.dram_tensor(n, list(sh), dt, kind="ExternalInput").ap()
    x_all = d("x_all", (S, H), F32)
    xTsp = d("xTsp", (32, H, 128), F32)
    Wqc_sl = d("Wqc_sl", (12, H, 128), F32)
    bqc = d("bqc", (128, 12), F32)
    WqupT = d("WqupT", (QL, 6144), F32)
    bqup = d("bqup", (128, 48), F32)
    WkvcT = d("WkvcT", (H, 576), F32)
    bkvc = d("bkvc", (128, 5), F32)
    WkunT = d("WkunT", (KVL, 4096), F32)
    bkun = d("bkun", (128, 32), F32)
    WkuvT = d("WkuvT", (KVL, 4096), F32)
    bkuv = d("bkuv", (1, 4096), F32)
    WoT = d("WoT", (128, H), F32)
    bo_r = d("bo_r", (1, H), F32)
    WbT = d("WbT", (H, H), F32)
    bb_r = d("bb_r", (1, H), F32)
    encT = d("encT", (H, 40), F32)
    co0 = d("co0", (3, 128, 1024), F32)
    co1 = d("co1", (3, 128, 256), F32)
    mk0 = d("mk0", (3, 128, 1024), U8)
    mk1 = d("mk1", (3, 128, 256), U8)
    FTb = d("FTb", (H, H), BF16)
    identf = d("identf", (128, 128), F32)
    y = nc.dram_tensor("y", [TM, H], F32, kind="ExternalOutput").ap()
    xnsp = nc.dram_tensor("xnsp", [128, 32 * 1024], BF16, kind="Internal").ap()

    al = _Alloc(nc)

    al.open()
    eT = al.sb("eT", [40, 1024], BF16)
    G = al.sb("G", [40, 4096], BF16)
    osumT = al.sb("osumT", [128, 1024], BF16)
    ones_rf = al.sb("ones_rf", [1, 128], F32)
    ones_rb = al.sb("ones_rb", [1, 128], BF16)
    ones_cb = al.sb("ones_cb", [128, 1], BF16)
    epsc = al.sb("epsc", [128, 1], F32)

    al.open()
    q_cT = al.sb("q_cT", [128, 12 * 1024], BF16)
    kv_nT = al.sb("kv_nT", [128, 4 * 4096], BF16)
    ropeT2 = al.sb("ropeT2", [128, 4096], BF16)
    o_sum = al.sb("o_sum", [128, 1024], F32)
    identf_sb = al.sb("identf_sb", [128, 128], F32)
    qrow_sp = [al.sb(f"qrow_sp{i}", [1, 128], F32) for i in range(2)]
    msq_t = al.sb("msq_t", [1, 128], F32)
    sd_t = al.sb("sd_t", [1, 128], F32)
    nrt_t = al.sb("nrt_t", [1, 128], F32)
    m8_row = al.sb("m8_row", [1, 1024], BF16)
    nr8_row = al.sb("nr8_row", [1, 1024], BF16)
    bqc_sb = al.sb("bqc_sb", [128, 12], F32)
    bqup_sb = al.sb("bqup_sb", [128, 48], F32)
    bkvc_sb = al.sb("bkvc_sb", [128, 5], F32)
    bkun_sb = al.sb("bkun_sb", [128, 32], F32)
    bkuv_b = al.sb("bkuv_b", [1, 4096], BF16)
    S_col = al.sb("S_col", [128, 5], F32)
    mb_sp = [al.sb(f"mb_sp{i}", [128, 128], BF16) for i in range(2)]
    nr_sp = [al.sb(f"nr_sp{i}", [128, 128], BF16) for i in range(2)]
    mrow_sp = [al.sb(f"mrow_sp{i}", [1, 128], F32) for i in range(2)]
    nrrow_sp = [al.sb(f"nrrow_sp{i}", [1, 128], F32) for i in range(2)]
    tmp1 = al.sb("tmp1", [128, 128], F32)
    tmp2 = al.sb("tmp2", [128, 128], F32)
    rc_col = al.sb("rc_col", [128, 1], F32)

    al.open()
    Wkvc_bf = al.sb("Wkvc_bf", [128, 32 * 576], BF16)
    kvw_st = [al.sb(f"kvw_st{i}", [128, 576], F32) for i in range(2)]

    al.open()
    FTbuf = [al.sb(f"FTbuf{i}", [128, 4096], BF16) for i in range(2)]
    costg = al.sb("costg", [128, 1024], F32)
    mstg = al.sb("mstg", [128, 1024], U8)
    mstgf = al.sb("mstgf", [128, 1024], F32)
    A0acc = al.sb("A0acc", [128, 1024], F32)
    A1acc = al.sb("A1acc", [128, 256], F32)
    A_bf = al.sb("A_bf", [128, 32 * 40], BF16)
    al.close()

    al.open()
    xt_st = [al.sb(f"xt_st{i}", [128, 1024], F32) for i in range(4)]
    xt_bf = [al.sb(f"xt_bf{i}", [128, 32 * 128], BF16) for i in range(3)]
    sq_sp = al.sb("sq_sp", [128, 4096], BF16)
    al.close()
    al.close()

    al.open()
    xnT = al.sb("xnT", [128, 32 * 1024], BF16)
    xt2_st = [al.sb(f"xt2_st{i}", [128, 1024], F32) for i in range(2)]
    qw_st = al.sb("qw_st", [128, 32 * 128], F32)
    qw_bf = [al.sb(f"qw_bf{i}", [128, 32 * 128], BF16) for i in range(1)]
    encT_b = al.sb("encT_b", [128, 32 * 40], BF16)
    al.close()

    al.open()
    qT_n1 = al.sb("qT_n1", [128, 1024], BF16)
    qT_n2 = al.sb("qT_n2", [128, 1024], BF16)
    qT_rp = al.sb("qT_rp", [128, 1024], BF16)
    qup_st = [al.sb(f"qup_st{i}", [128, 384], F32) for i in range(2)]
    qup_bf = [al.sb(f"qup_bf{i}", [128, 12 * 384], BF16) for i in range(2)]
    kT_buf = al.sb("kT_buf", [128, 4096], BF16)
    vbuf = al.sb("vbuf", [128, 32 * 258], BF16)
    kuw_st = al.sb("kuw_st", [128, 2048], F32)
    kuw_bf = al.sb("kuw_bf", [128, 2048], BF16)
    pT = [al.sb(f"pT{i}", [128, 1024], BF16) for i in range(2)]
    al.close()

    al.close()

    al.open()
    xnT2 = al.sb("xnT2", [128, 32 * 1024], BF16)
    wb_st = [al.sb(f"wb_st{i}", [128, 512], F32) for i in range(2)]
    wb_bf = [al.sb(f"wb_bf{i}", [128, 32 * 512], BF16) for i in range(2)]
    WoT_b = al.sb("WoT_b", [128, 4096], BF16)
    xres = [al.sb(f"xres{i}", [128, 512], F32) for i in range(2)]
    out_sb = [al.sb(f"out_sb{i}", [128, 512], F32) for i in range(2)]
    bias_f = al.sb("bias_f", [1, 4096], F32)
    bias_f2 = al.sb("bias_f2", [1, 4096], F32)
    bias_cb = al.sb("bias_cb", [1, 4096], BF16)
    al.close()

    al.close()

    from contextlib import ExitStack
    with ExitStack() as ctx:
        E = ctx.enter_context
        psA = E(nc.psum_tensor("psA", [128, 1024], F32))
        psB = E(nc.psum_tensor("psB", [128, 1024], F32))
        psC = E(nc.psum_tensor("psC", [128, 1024], F32))
        psD = E(nc.psum_tensor("psD", [128, 1024], F32))
        AB = [psA, psB]
        CD = [psC, psD]
        GREG = [psA, psA, psB, psB, psC, psC, psD, psD]

        sems = {}
        for name in ("s_const", "s_coef", "s_ft0", "s_ft1", "s_bkv",
                     "s_kvw0", "s_kvw1", "s_xt0", "s_xt1", "s_xt2", "s_xt3", "s_xt2a", "s_xt2b", "s_qw",
                     "s_enc", "s_spill", "s_rope", "s_qup0", "s_qup1",
                     "s_kuw", "s_xnld", "s_bias", "s_wot0", "s_wot1",
                     "s_wb0", "s_wb1", "s_xr0", "s_xr1", "s_go0", "s_go1",
                     "ve", "t_ftu", "s_gev", "v_abf", "v_init",
                     "v_misc", "s_sq", "t_st1", "s_stA", "v_stB", "s_stC",
                     "v_stD", "t_bc", "v_bc", "ct_kw",
                     "t_s", "v_sev", "v_xt", "t_kv", "v_kvev", "ct_qw",
                     "t_qc", "v_qev", "ct_enc", "t_e", "v_eev", "ct_qup",
                     "ct_kuw", "t_bld", "v_bld", "t_sc", "s_exp", "t_pv",
                     "v_fl", "t_tr", "v_osT", "v_grow", "ct_wb", "t_out",
                     "v_oev", "ct_wot"):
            sems[name] = E(nc.semaphore(name))
        sm = type("SM", (), sems)

        LD = {}
        block = E(nc.Block())

        @block.sync
        def _(sync):
            cnt = {}

            def dma(name, out, in_, sem):
                sync.dma_start(out, in_).then_inc(sem, 16)
                cnt[sem] = cnt.get(sem, 0) + 16
                LD[name] = (sem, cnt[sem])

            dma("identf", identf_sb[:], identf, sm.s_const)
            dma("bqc", bqc_sb[:], bqc, sm.s_const)
            dma("bqup", bqup_sb[:], bqup, sm.s_const)
            dma("bkvc", bkvc_sb[:], bkvc, sm.s_const)
            dma("bkun", bkun_sb[:], bkun, sm.s_const)
            for e in range(3):
                if e >= 1:
                    sync.wait_ge(sm.v_init, 1 + e)
                dma(f"co0_{e}", costg[:], co0[e], sm.s_coef)
                dma(f"mk0_{e}", mstg[:], mk0[e], sm.s_coef)
            for e in range(3):
                sync.wait_ge(sm.v_init, 4 + e)
                dma(f"co1_{e}", costg[:, 0:256], co1[e], sm.s_coef)
                dma(f"mk1_{e}", mstg[:, 0:256], mk1[e], sm.s_coef)
            for k in range(32):
                if k >= 2:
                    sync.wait_ge(sm.ct_kw, k - 1)
                dma(f"kvw{k}", kvw_st[k % 2][:],
                    WkvcT[128 * k:128 * (k + 1), :],
                    sm.s_kvw0 if k % 2 == 0 else sm.s_kvw1)
            sync.wait_ge(sm.v_abf, 1)
            sync.wait_ge(sm.t_ftu, 32)
            for s in range(32):
                for g in range(4):
                    gi = 4 * s + g
                    if gi >= 4:
                        sync.wait_ge(sm.v_xt, gi - 3)
                    dma(f"xt_{s}_{g}",
                        xt_st[gi % 4][:].rearrange("p (c t) -> p c t", c=8),
                        xTsp[s].rearrange("(c p) t -> p c t", p=128
                                          )[:, 8 * g:8 * (g + 1), :],
                        [sm.s_xt0, sm.s_xt1, sm.s_xt2, sm.s_xt3][gi % 4])
            sync.wait_ge(sm.t_kv, 160)
            for s in range(8):
                for g in range(4):
                    gi2 = 4 * s + g
                    if gi2 >= 2:
                        sync.wait_ge(sm.v_xt, 128 + gi2 - 1)
                    dma(f"xt2_{s}_{g}",
                        xt2_st[gi2 % 2][:].rearrange("p (c t) -> p c t", c=8),
                        xTsp[s].rearrange("(c p) t -> p c t", p=128
                                          )[:, 8 * g:8 * (g + 1), :],
                        sm.s_xt2a if gi2 % 2 == 0 else sm.s_xt2b)
            for l in range(12):
                if l >= 1:
                    sync.wait_ge(sm.ct_qw, l)
                dma(f"qw{l}", qw_st[:].rearrange("p (c t) -> p c t", c=32),
                    Wqc_sl[l].rearrange("(c p) n -> p c n", p=128), sm.s_qw)
            sync.wait_ge(sm.ct_qw, 12)
            dma("enc", qw_st[:, 0:1280].rearrange("p (c r) -> p c r", c=32),
                encT.rearrange("(c p) r -> p c r", p=128), sm.s_enc)
            sync.wait_ge(sm.ct_enc, 1)
            dma("bkuv", qw_st[0:1, 0:4096], bkuv, sm.s_bkv)
            sync.wait_ge(sm.t_e, 2)
            for i in range(8):
                dma(f"spill{i}", xnsp[:, 4096 * i:4096 * (i + 1)],
                    xnT[:, 4096 * i:4096 * (i + 1)], sm.s_spill)
            sync.wait_ge(sm.v_kvev, 160)
            dma("rope_dup", ropeT2[64:128, :], ropeT2[0:64, :], sm.s_rope)
            sync.wait_ge(sm.t_qc, 24)
            sync.wait_ge(sm.s_spill, 128)
            for p in range(NP):
                for c in range(12):
                    gq = 12 * p + c
                    if gq >= 2:
                        sync.wait_ge(sm.ct_qup, gq - 1)
                    if p >= 2 and c == 0:
                        sync.wait_ge(sm.t_bld, 30 * (p - 1))
                    dma(f"qup_{p}_{c}", qup_st[c % 2][:],
                        WqupT[128 * c:128 * (c + 1), 384 * p:384 * (p + 1)],
                        sm.s_qup0 if c % 2 == 0 else sm.s_qup1)
                if p >= 1:
                    sync.wait_ge(sm.ct_kuw, p)
                dma(f"kun_{p}",
                    kuw_st[:, 0:1024].rearrange("p (c t) -> p c t", c=4),
                    WkunT[:, 256 * p:256 * (p + 1)].rearrange(
                        "(c p) n -> p c n", p=128), sm.s_kuw)
                dma(f"kuv_{p}",
                    kuw_st[:, 1024:2048].rearrange("p (c t) -> p c t", c=4),
                    WkuvT[:, 256 * p:256 * (p + 1)].rearrange(
                        "(c p) n -> p c n", p=128), sm.s_kuw)
            sync.wait_ge(sm.v_osT, 8)
            for i in range(8):
                dma(f"xnld{i}", xnT2[:, 4096 * i:4096 * (i + 1)],
                    xnsp[:, 4096 * i:4096 * (i + 1)], sm.s_xnld)
            dma("bo", bias_f[:], bo_r, sm.s_bias)
            dma("bb", bias_f2[:], bb_r, sm.s_bias)
            for i in range(8):
                if i >= 2:
                    sync.wait_ge(sm.ct_wot, i - 1)
                dma(f"wot{i}", wb_st[i % 2][:],
                    WoT[:, 512 * i:512 * (i + 1)],
                    sm.s_wot0 if i % 2 == 0 else sm.s_wot1)
            sync.wait_ge(sm.ct_wot, 8)
            for k in range(32):
                if k >= 2:
                    sync.wait_ge(sm.ct_wb, k - 1)
                dma(f"wb_0_{k}", wb_st[k % 2][:],
                    WbT[128 * k:128 * (k + 1), 0:512],
                    sm.s_wb0 if k % 2 == 0 else sm.s_wb1)
            for s in range(8):
                for t in range(8):
                    idx = 8 * s + t
                    if idx >= 2:
                        sync.wait_ge(sm.v_oev, idx - 1)
                    dma(f"xres_{s}_{t}", xres[idx % 2][:],
                        x_all[128 * t:128 * (t + 1), 512 * s:512 * (s + 1)],
                        sm.s_xr0 if idx % 2 == 0 else sm.s_xr1)
                    if s + 1 < 8:
                        for k in range(4 * t, 4 * t + 4):
                            ck = 32 * (s + 1) + k
                            sync.wait_ge(sm.ct_wb, ck - 1)
                            dma(f"wb_{s+1}_{k}", wb_st[k % 2][:],
                                WbT[128 * k:128 * (k + 1),
                                    512 * (s + 1):512 * (s + 2)],
                                sm.s_wb0 if k % 2 == 0 else sm.s_wb1)

        @block.gpsimd
        def _(gp):
            for idx in range(64):
                s, t = idx // 8, idx % 8
                gp.wait_ge(sm.v_oev, idx + 1)
                gp.dma_start(y[128 * t:128 * (t + 1), 512 * s:512 * (s + 1)],
                             out_sb[idx % 2][:]).then_inc(
                    sm.s_go0 if idx % 2 == 0 else sm.s_go1, 16)
            gp.wait_ge(sm.s_go0, 16 * 32)
            gp.wait_ge(sm.s_go1, 16 * 32)

        @block.tensor
        def _(tensor):
            tensor.wait_ge(sm.v_abf, 1)
            for k in range(32):
                tensor.wait_ge(sm.s_ft0 if k % 2 == 0 else sm.s_ft1,
                               16 * (k // 2 + 1))
                for sp in range(8):
                    reg = GREG[sp]
                    col = (sp % 2) * 512
                    mm = tensor.matmul(reg[0:40, col:col + 512],
                                       A_bf[:, 40 * k:40 * (k + 1)],
                                       FTbuf[k % 2][:, 512 * sp:512 * (sp + 1)],
                                       start=(k == 0), stop=(k == 31))
                mm.then_inc(sm.t_ftu, 1)
            tensor.wait_ge(sm.s_gev, 8)
            tensor.wait_ge(sm.ct_kw, 32)
            tensor.wait_ge(sm.v_init, 1)
            for m in range(5):
                msz = 128 if m < 4 else 64
                if m >= 2:
                    tensor.wait_ge(sm.v_sev, m - 1)
                for k in range(32):
                    mm = tensor.matmul(
                        CD[m % 2][0:msz, 512:513],
                        Wkvc_bf[:, 576 * k + 128 * m:576 * k + 128 * m + msz],
                        ones_cb[:], start=(k == 0), stop=(k == 31))
                mm.then_inc(sm.t_s, 1)
            tensor.wait_ge(sm.s_const, 80)
            tensor.wait_ge(sm.v_sev, 5)

            def emit_stats(sp2):
                tensor.wait_ge(sm.v_xt, 4 * (sp2 + 1))
                if sp2 >= 2:
                    tensor.wait_ge(sm.s_stA, sp2 - 1)
                reg = CD[sp2 % 2]
                for k in range(32):
                    mm = tensor.matmul(
                        reg[0:1, 512:640], ones_cb[:],
                        xt_bf[sp2 % 3][:, 128 * k:128 * (k + 1)],
                        start=(k == 0), stop=(k == 31))
                tensor.wait_ge(sm.s_sq, sp2 + 1)
                for k in range(32):
                    mm = tensor.matmul(
                        reg[0:1, 640:768], ones_cb[:],
                        sq_sp[:, 128 * k:128 * (k + 1)],
                        start=(k == 0), stop=(k == 31))
                mm.then_inc(sm.t_st1, 1)

            emit_stats(0)
            for s in range(32):
                if s + 1 < 32:
                    emit_stats(s + 1)
                tensor.wait_ge(sm.v_stD, s + 1)
                if 2 * s >= 2:
                    tensor.wait_ge(sm.v_bc, 2 * s - 1)
                tensor.matmul(CD[0][0:128, 0:128], ones_rf[:],
                              mrow_sp[s % 2][:],
                              start=True, stop=True).then_inc(sm.t_bc, 1)
                if 2 * s + 1 >= 2:
                    tensor.wait_ge(sm.v_bc, 2 * s)
                tensor.matmul(CD[1][0:128, 0:128], ones_rf[:],
                              nrrow_sp[s % 2][:],
                              start=True, stop=True).then_inc(sm.t_bc, 1)
                for m in range(5):
                    i = 5 * s + m
                    msz = 128 if m < 4 else 64
                    kreg, kcol = KV_REG4[i % 4]
                    if i >= 4:
                        tensor.wait_ge(sm.v_kvev, i - 3)
                    for k in range(32):
                        mm = tensor.matmul(
                            AB[kreg][0:msz, kcol:kcol + 128],
                            Wkvc_bf[:, 576 * k + 128 * m:576 * k + 128 * m + msz],
                            xt_bf[s % 3][:, 128 * k:128 * (k + 1)],
                            start=(k == 0), stop=(k == 31))
                    mm.then_inc(sm.t_kv, 1)
            for s in range(8):
                for w in range(2):
                    bi = 64 + 2 * s + w
                    tensor.wait_ge(sm.v_bc, bi - 1)
                    row = m8_row if w == 0 else nr8_row
                    tensor.matmul(CD[w][0:128, 0:128], ones_rb[:],
                                  row[0:1, 128 * s:128 * (s + 1)],
                                  start=True, stop=True).then_inc(sm.t_bc, 1)
            for l in range(12):
                tensor.wait_ge(sm.ct_qw, l + 1)
                for sp in range(2):
                    i = 2 * l + sp
                    tensor.wait_ge(sm.v_xt, 128 + 16 * (sp + 1))
                    if i >= 2:
                        tensor.wait_ge(sm.v_qev, i - 1)
                    for k in range(32):
                        mm = tensor.matmul(
                            AB[i % 2][:, 0:512],
                            qw_bf[0][:, 128 * k:128 * (k + 1)],
                            xnT[:, 1024 * k + 512 * sp:1024 * k + 512 * sp + 512],
                            start=(k == 0), stop=(k == 31))
                    mm.then_inc(sm.t_qc, 1)
            tensor.wait_ge(sm.ct_enc, 1)
            tensor.wait_ge(sm.v_bc, 80)
            for sp in range(2):
                for k in range(32):
                    mm = tensor.matmul(
                        CD[sp][0:40, 0:512],
                        encT_b[:, 40 * k:40 * (k + 1)],
                        xnT[:, 1024 * k + 512 * sp:1024 * k + 512 * sp + 512],
                        start=(k == 0), stop=(k == 31))
                mm.then_inc(sm.t_e, 1)
            tensor.wait_ge(sm.v_qev, 24)
            tensor.wait_ge(sm.v_eev, 2)
            tensor.wait_ge(*LD["rope_dup"])
            tensor.wait_ge(sm.v_misc, 1)
            for p in range(NP):
                tensor.wait_ge(sm.ct_qup, 12 * (p + 1))
                for gq in range(6):
                    bi = 30 * p + gq
                    if gq == 0 and p > 0:
                        tensor.wait_ge(sm.s_exp, 64 * p - 1)
                    if gq == 1 and p > 0:
                        tensor.wait_ge(sm.s_exp, 64 * p)
                    if bi >= 2:
                        tensor.wait_ge(sm.v_bld, bi - 1)
                    mch, sp = gq // 2, gq % 2
                    for k in range(12):
                        mm = tensor.matmul(
                            AB[bi % 2][:, 0:512],
                            qup_bf[p % 2][:, 384 * k + 128 * mch:
                                          384 * k + 128 * mch + 128],
                            q_cT[:, 1024 * k + 512 * sp:1024 * k + 512 * sp + 512],
                            start=(k == 0), stop=(k == 11))
                    mm.then_inc(sm.t_bld, 1)
                tensor.wait_ge(sm.ct_kuw, p + 1)
                for r in range(8):
                    bi = 30 * p + 6 + r
                    if bi >= 2:
                        tensor.wait_ge(sm.v_bld, bi - 1)
                    for wl in range(4):
                        w = 4 * r + wl
                        col = 256 * wl
                        for rc in range(4):
                            tensor.matmul(
                                AB[bi % 2][:, col:col + 256],
                                kv_nT[:, 4096 * rc + 128 * w:
                                      4096 * rc + 128 * w + 128],
                                kuw_bf[:, 1024 + 256 * rc:1024 + 256 * (rc + 1)],
                                start=(rc == 0), stop=False)
                        mm = tensor.matmul(
                            AB[bi % 2][:, col:col + 256], ones_rb[:],
                            bkuv_b[0:1, 256 * p:256 * (p + 1)],
                            start=False, stop=True)
                    mm.then_inc(sm.t_bld, 1)
                for hh in range(2):
                    h = 2 * p + hh
                    for sp in range(8):
                        bi = 30 * p + 14 + 8 * hh + sp
                        if sp == 0 and hh == 1:
                            tensor.wait_ge(sm.s_exp, 32 * h - 1)
                        if sp == 1 and hh == 1:
                            tensor.wait_ge(sm.s_exp, 32 * h)
                        if bi >= 2:
                            tensor.wait_ge(sm.v_bld, bi - 1)
                        for rc in range(4):
                            mm = tensor.matmul(
                                AB[bi % 2][:, 0:512],
                                kuw_bf[:, 256 * rc + 128 * hh:
                                       256 * rc + 128 * hh + 128],
                                kv_nT[:, 4096 * rc + 512 * sp:
                                      4096 * rc + 512 * sp + 512],
                                start=(rc == 0), stop=(rc == 3))
                        mm.then_inc(sm.t_bld, 1)
                    qn = qT_n1 if hh == 0 else qT_n2
                    rbase = 64 * hh

                    def emit_scores(kwin):
                        idx = 32 * h + kwin
                        if kwin < 2:
                            tensor.wait_ge(sm.v_bld, 30 * p + 14 + 8 * (hh + 1))
                        else:
                            tensor.wait_ge(sm.s_exp, idx - 1)
                        reg = AB[kwin % 2]
                        for sp in range(2):
                            tensor.matmul(reg[:, 512 * sp:512 * (sp + 1)],
                                          kT_buf[:, 128 * kwin:128 * (kwin + 1)],
                                          qn[:, 512 * sp:512 * (sp + 1)],
                                          start=True, stop=False)
                            mm = tensor.matmul(
                                reg[:, 512 * sp:512 * (sp + 1)],
                                ropeT2[rbase:rbase + 64,
                                       128 * kwin:128 * (kwin + 1)],
                                qT_rp[rbase:rbase + 64, 512 * sp:512 * (sp + 1)],
                                start=False, stop=True)
                        mm.then_inc(sm.t_sc, 1)

                    def emit_pv(kwin):
                        idx = 32 * h + kwin
                        tensor.wait_ge(sm.s_exp, idx + 1)
                        if kwin == 0 and h > 0:
                            for qt in range(8):
                                tensor.wait_ge(sm.v_fl,
                                               8 * (h - 1) + FL_TH[qt])
                        for qt in range(8):
                            regi, col = O_COL[qt]
                            mm = tensor.matmul(
                                CD[regi][:, col:col + 129],
                                pT[kwin % 2][:, 128 * qt:128 * (qt + 1)],
                                vbuf[:, 258 * kwin + 129 * hh:
                                     258 * kwin + 129 * hh + 129],
                                start=(kwin == 0 and qt in (0, 3, 6)),
                                stop=(kwin == 31 and qt in (2, 5, 7)))
                        mm.then_inc(sm.t_pv, 1)

                    emit_scores(0)
                    emit_scores(1)
                    for kwin in range(32):
                        if kwin + 2 < 32:
                            emit_scores(kwin + 2)
                        emit_pv(kwin)
            tensor.wait_ge(sm.v_fl, 8 * NH)
            for qt in range(8):
                if qt >= 2:
                    tensor.wait_ge(sm.v_osT, qt - 1)
                base = 512 + 128 * (qt % 2)
                tensor.transpose(psD[:, base:base + 128],
                                 o_sum[:, 128 * qt:128 * (qt + 1)],
                                 identf_sb[:]).then_inc(sm.t_tr, 1)
            tensor.wait_ge(sm.v_osT, 8)
            tensor.wait_ge(sm.v_grow, 1)
            tensor.wait_ge(sm.s_xnld, 128)
            for s in range(8):
                tensor.wait_ge(sm.ct_wb, 32 * (s + 1))
                for t in range(8):
                    idx = 8 * s + t
                    if idx >= 2:
                        tensor.wait_ge(sm.v_oev, idx - 1)
                    reg = AB[idx % 2]
                    for k in range(32):
                        tensor.matmul(
                            reg[:, 0:512],
                            xnT2[:, 1024 * k + 128 * t:1024 * k + 128 * t + 128],
                            wb_bf[s % 2][:, 512 * k:512 * (k + 1)],
                            start=(k == 0), stop=False)
                    tensor.matmul(reg[:, 0:512], eT[:, 128 * t:128 * (t + 1)],
                                  G[:, 512 * s:512 * (s + 1)],
                                  start=False, stop=False)
                    tensor.matmul(reg[:, 0:512], ones_rb[:],
                                  bias_cb[0:1, 512 * s:512 * (s + 1)],
                                  start=False, stop=False)
                    tensor.matmul(reg[:, 0:512], osumT[:, 128 * t:128 * (t + 1)],
                                  WoT_b[:, 512 * s:512 * (s + 1)],
                                  start=False, stop=True).then_inc(sm.t_out, 1)

        @block.scalar
        def _(scalar):
            for j in range(32):
                if j >= 2:
                    scalar.wait_ge(sm.t_ftu, j - 1)
                scalar.dma_start(FTbuf[j % 2][:],
                                 FTb[128 * j:128 * (j + 1), :]).then_inc(
                    sm.s_ft0 if j % 2 == 0 else sm.s_ft1, 16)
            scalar.wait_ge(sm.t_ftu, 32)
            for sp in range(8):
                reg = GREG[sp]
                col = (sp % 2) * 512
                scalar.activation(G[0:40, 512 * sp:512 * (sp + 1)],
                                  reg[0:40, col:col + 512], AF.Copy,
                                  scale=0.5).then_inc(sm.s_gev, 1)
            scalar.wait_ge(sm.v_xt, 4)
            scalar.activation(sq_sp[:], xt_bf[0][:],
                              AF.Square).then_inc(sm.s_sq, 1)
            for sspan in range(32):
                if sspan + 1 < 32:
                    scalar.wait_ge(sm.v_xt, 4 * (sspan + 2))
                    scalar.wait_ge(sm.t_st1, sspan + 1)
                    scalar.activation(sq_sp[:], xt_bf[(sspan + 1) % 3][:],
                                      AF.Square).then_inc(sm.s_sq, 1)
                scalar.wait_ge(sm.t_st1, sspan + 1)
                if sspan >= 2:
                    scalar.wait_ge(sm.t_bc, 2 * sspan - 2)
                    scalar.wait_ge(sm.v_stB, sspan - 1)
                reg = CD[sspan % 2]
                scalar.activation(mrow_sp[sspan % 2][:], reg[0:1, 512:640],
                                  AF.Copy, scale=1.0 / H)
                scalar.activation(qrow_sp[sspan % 2][:], reg[0:1, 640:768],
                                  AF.Copy, scale=1.0 / H).then_inc(sm.s_stA, 1)
                scalar.wait_ge(sm.v_stB, sspan + 1)
                scalar.activation(sd_t[:], qrow_sp[sspan % 2][:], AF.Sqrt,
                                  bias=epsc[0:1, :]).then_inc(sm.s_stC, 1)
            for h in range(NH):
                for kwin in range(32):
                    idx = 32 * h + kwin
                    scalar.wait_ge(sm.t_sc, idx + 1)
                    if idx >= 2:
                        scalar.wait_ge(sm.t_pv, idx - 1)
                    scalar.activation(pT[kwin % 2][:], AB[kwin % 2][:, 0:1024],
                                      AF.Exp, scale=ISCALE).then_inc(sm.s_exp, 1)

        @block.vector
        def _(vector):
            nve = [0]

            def vedge(ins):
                ins.then_inc(sm.ve, 1)
                nve[0] += 1
                vector.wait_ge(sm.ve, nve[0])

            vector.memset(ones_rf[:], 1.0)
            vector.memset(ones_rb[:], 1.0)
            vector.memset(ones_cb[:], 1.0)
            vector.memset(epsc[:], 1e-5)
            vector.memset(o_sum[:], 0.0).then_inc(sm.v_init, 1)
            vector.wait_ge(sm.s_const, 80)
            for e in range(3):
                vector.wait_ge(*LD[f"mk0_{e}"])
                vedge(vector.tensor_copy(mstgf[:], mstg[:]))
                if e == 0:
                    vedge(vector.tensor_tensor(A0acc[:], costg[:], mstgf[:],
                                               ALU.mult))
                else:
                    vedge(vector.tensor_tensor(mstgf[:], costg[:], mstgf[:],
                                               ALU.mult))
                    vedge(vector.tensor_add(A0acc[:], A0acc[:], mstgf[:]))
                vector.engine_nop().then_inc(sm.v_init, 1)
            for e in range(3):
                vector.wait_ge(*LD[f"mk1_{e}"])
                vedge(vector.tensor_copy(mstgf[:, 0:256], mstg[:, 0:256]))
                if e == 0:
                    vedge(vector.tensor_tensor(A1acc[:], costg[:, 0:256],
                                               mstgf[:, 0:256], ALU.mult))
                else:
                    vedge(vector.tensor_tensor(mstgf[:, 0:256], costg[:, 0:256],
                                               mstgf[:, 0:256], ALU.mult))
                    vedge(vector.tensor_add(A1acc[:], A1acc[:],
                                            mstgf[:, 0:256]))
                vector.engine_nop().then_inc(sm.v_init, 1)
            for c in range(32):
                vector.tensor_copy(A_bf[:, 40 * c:40 * c + 32],
                                   A0acc[:, 32 * c:32 * (c + 1)])
                vector.tensor_copy(A_bf[:, 40 * c + 32:40 * c + 40],
                                   A1acc[:, 8 * c:8 * (c + 1)])
            vector.engine_nop().then_inc(sm.v_abf, 1)
            for k in range(32):
                vector.wait_ge(*LD[f"kvw{k}"])
                vector.tensor_copy(Wkvc_bf[:, 576 * k:576 * (k + 1)],
                                   kvw_st[k % 2][:]).then_inc(sm.ct_kw, 1)
            for m in range(5):
                msz = 128 if m < 4 else 64
                vector.wait_ge(sm.t_s, m + 1)
                vector.tensor_copy(S_col[0:msz, m:m + 1],
                                   CD[m % 2][0:msz, 512:513]).then_inc(sm.v_sev, 1)
            def emit_casts(sp2):
                for g in range(4):
                    gi = 4 * sp2 + g
                    vector.wait_ge(*LD[f"xt_{sp2}_{g}"])
                    if sp2 >= 3 and g == 0:
                        vector.wait_ge(sm.t_kv, 5 * (sp2 - 2))
                    vector.tensor_copy(
                        xt_bf[sp2 % 3][:, 1024 * g:1024 * (g + 1)],
                        xt_st[gi % 4][:]).then_inc(sm.v_xt, 1)

            emit_casts(0)
            for s in range(32):
                if s + 1 < 32:
                    emit_casts(s + 1)
                vector.wait_ge(sm.s_stA, s + 1)
                vedge(vector.tensor_tensor(msq_t[:], mrow_sp[s % 2][:],
                                           mrow_sp[s % 2][:], ALU.mult))
                vector.tensor_sub(qrow_sp[s % 2][:], qrow_sp[s % 2][:],
                                  msq_t[:]).then_inc(sm.v_stB, 1)
                vector.wait_ge(sm.s_stC, s + 1)
                vedge(vector.reciprocal(nrt_t[:], sd_t[:]))
                vedge(vector.tensor_scalar_mul(nrrow_sp[s % 2][:], nrt_t[:],
                                               -1.0))
                if s < 8:
                    vector.tensor_copy(m8_row[0:1, 128 * s:128 * (s + 1)],
                                       mrow_sp[s % 2][:])
                    vector.tensor_copy(nr8_row[0:1, 128 * s:128 * (s + 1)],
                                       nrrow_sp[s % 2][:])
                vector.engine_nop().then_inc(sm.v_stD, 1)
                for w in range(2):
                    bi = 2 * s + w
                    vector.wait_ge(sm.t_bc, bi + 1)
                    dst = mb_sp[s % 2] if w == 0 else nr_sp[s % 2]
                    vedge(vector.tensor_copy(dst[:], CD[w][0:128, 0:128]))
                    vector.engine_nop().then_inc(sm.v_bc, 1)
                for m in range(5):
                    i = 5 * s + m
                    msz = 128 if m < 4 else 64
                    kreg, kcol = KV_REG4[i % 4]
                    vector.wait_ge(sm.t_kv, i + 1)
                    vedge(vector.scalar_tensor_tensor(
                        tmp1[0:msz, :], mb_sp[s % 2][0:msz, :],
                        S_col[0:msz, m:m + 1],
                        AB[kreg][0:msz, kcol:kcol + 128],
                        ALU.mult, ALU.subtract))
                    vedge(vector.tensor_mul(tmp2[0:msz, :], tmp1[0:msz, :],
                                            nr_sp[s % 2][0:msz, :]))
                    if m < 4:
                        dst = kv_nT[:, 4096 * m + 128 * s:4096 * m + 128 * s + 128]
                        vector.tensor_scalar(dst, tmp2[:],
                                             bkvc_sb[:, m:m + 1],
                                             None, ALU.add).then_inc(sm.v_kvev, 1)
                    else:
                        dst = ropeT2[0:64, 128 * s:128 * (s + 1)]
                        vector.tensor_scalar(dst, tmp2[0:64, :],
                                             bkvc_sb[0:64, m:m + 1], None,
                                             ALU.add).then_inc(sm.v_kvev, 1)
            for s in range(8):
                for w in range(2):
                    bi = 64 + 2 * s + w
                    vector.wait_ge(sm.t_bc, bi + 1)
                    dst = mb_sp[s % 2] if w == 0 else nr_sp[s % 2]
                    vedge(vector.tensor_copy(dst[:], CD[w][0:128, 0:128]))
                    vector.engine_nop().then_inc(sm.v_bc, 1)
                for g in range(4):
                    gi2 = 4 * s + g
                    vector.wait_ge(*LD[f"xt2_{s}_{g}"])
                    for ci in range(8):
                        k = 8 * g + ci
                        dst = xnT[:, 1024 * k + 128 * s:1024 * k + 128 * s + 128]
                        vedge(vector.tensor_sub(
                            tmp1[:], mb_sp[s % 2][:],
                            xt2_st[gi2 % 2][:, 128 * ci:128 * (ci + 1)]))
                        vedge(vector.tensor_mul(dst, tmp1[:], nr_sp[s % 2][:]))
                    vector.engine_nop().then_inc(sm.v_xt, 1)
            for l in range(12):
                vector.wait_ge(*LD[f"qw{l}"])
                if l >= 1:
                    vector.wait_ge(sm.t_qc, 2 * l)
                vector.tensor_copy(qw_bf[0][:],
                                   qw_st[:]).then_inc(sm.ct_qw, 1)
                for sp in range(2):
                    i = 2 * l + sp
                    vector.wait_ge(sm.t_qc, i + 1)
                    vector.tensor_scalar(
                        q_cT[:, 1024 * l + 512 * sp:1024 * l + 512 * sp + 512],
                        AB[i % 2][:, 0:512], bqc_sb[:, l:l + 1], None,
                        ALU.add).then_inc(sm.v_qev, 1)
            vector.wait_ge(sm.s_enc, 16)
            vector.tensor_copy(encT_b[:], qw_st[:, 0:1280]).then_inc(sm.ct_enc, 1)
            vector.wait_ge(sm.s_bkv, 16)
            vector.tensor_copy(bkuv_b[:], qw_st[0:1, 0:4096]).then_inc(sm.v_misc, 1)
            for sp in range(2):
                vector.wait_ge(sm.t_e, sp + 1)
                vector.tensor_copy(eT[0:40, 512 * sp:512 * (sp + 1)],
                                   CD[sp][0:40, 0:512]).then_inc(sm.v_eev, 1)
            vector.wait_ge(sm.s_spill, 128)
            vector.memset(vbuf[:], 1.0)
            for p in range(NP):
                for c in range(12):
                    vector.wait_ge(*LD[f"qup_{p}_{c}"])
                    if p >= 2 and c == 0:
                        vector.wait_ge(sm.t_bld, 30 * (p - 1) + 6)
                    vector.tensor_copy(qup_bf[p % 2][:, 384 * c:384 * (c + 1)],
                                       qup_st[c % 2][:]).then_inc(sm.ct_qup, 1)
                vector.wait_ge(*LD[f"kuv_{p}"])
                if p >= 1:
                    vector.wait_ge(sm.t_bld, 30 * p)
                vector.tensor_copy(kuw_bf[:], kuw_st[:]).then_inc(sm.ct_kuw, 1)
                for gq in range(6):
                    bi = 30 * p + gq
                    vector.wait_ge(sm.t_bld, bi + 1)
                    mch, sp = gq // 2, gq % 2
                    dst = [qT_n1, qT_n2, qT_rp][mch]
                    vector.tensor_scalar(
                        dst[:, 512 * sp:512 * (sp + 1)], AB[bi % 2][:, 0:512],
                        bqup_sb[:, 3 * p + mch:3 * p + mch + 1], None,
                        ALU.add).then_inc(sm.v_bld, 1)
                for r in range(8):
                    bi = 30 * p + 6 + r
                    vector.wait_ge(sm.t_bld, bi + 1)
                    for wl in range(4):
                        w = 4 * r + wl
                        for hh in range(2):
                            vector.tensor_copy(
                                vbuf[:, 258 * w + 129 * hh:
                                     258 * w + 129 * hh + 128],
                                AB[bi % 2][:, 256 * wl + 128 * hh:
                                           256 * wl + 128 * hh + 128])
                    vector.engine_nop().then_inc(sm.v_bld, 1)
                for hh in range(2):
                    h = 2 * p + hh
                    for sp in range(8):
                        bi = 30 * p + 14 + 8 * hh + sp
                        vector.wait_ge(sm.t_bld, bi + 1)
                        vector.tensor_scalar(
                            kT_buf[:, 512 * sp:512 * (sp + 1)],
                            AB[bi % 2][:, 0:512], bkun_sb[:, h:h + 1], None,
                            ALU.add).then_inc(sm.v_bld, 1)
                    vector.wait_ge(sm.t_pv, 32 * (h + 1))
                    for qt in range(8):
                        regi, col = O_COL[qt]
                        oreg = CD[regi]
                        vedge(vector.reciprocal(rc_col[:],
                                                oreg[:, col + 128:col + 129]))
                        vedge(vector.tensor_scalar(tmp1[:],
                                                   oreg[:, col:col + 128],
                                                   rc_col[:], None, ALU.mult))
                        vector.tensor_add(o_sum[:, 128 * qt:128 * (qt + 1)],
                                          o_sum[:, 128 * qt:128 * (qt + 1)],
                                          tmp1[:]).then_inc(sm.v_fl, 1)
            for qt in range(8):
                vector.wait_ge(sm.t_tr, qt + 1)
                base = 512 + 128 * (qt % 2)
                vector.tensor_copy(osumT[:, 128 * qt:128 * (qt + 1)],
                                   psD[:, base:base + 128]).then_inc(sm.v_osT, 1)
            vector.wait_ge(sm.s_bias, 32)
            vedge(vector.tensor_add(bias_f[:], bias_f[:], bias_f2[:]))
            vector.tensor_copy(bias_cb[:], bias_f[:]).then_inc(sm.v_grow, 1)
            for i in range(8):
                vector.wait_ge(*LD[f"wot{i}"])
                vector.tensor_copy(WoT_b[:, 512 * i:512 * (i + 1)],
                                   wb_st[i % 2][:]).then_inc(sm.ct_wot, 1)
            for k in range(32):
                vector.wait_ge(*LD[f"wb_0_{k}"])
                vector.tensor_copy(wb_bf[0][:, 512 * k:512 * (k + 1)],
                                   wb_st[k % 2][:]).then_inc(sm.ct_wb, 1)
            for s in range(8):
                for t in range(8):
                    idx = 8 * s + t
                    if s + 1 < 8:
                        for k in range(4 * t, 4 * t + 4):
                            ck = 32 * (s + 1) + k
                            vector.wait_ge(*LD[f"wb_{s+1}_{k}"])
                            if s + 1 >= 2 and k == 0:
                                vector.wait_ge(sm.t_out, 8 * s)
                            vector.tensor_copy(
                                wb_bf[(s + 1) % 2][:, 512 * k:512 * (k + 1)],
                                wb_st[k % 2][:]).then_inc(sm.ct_wb, 1)
                    vector.wait_ge(sm.t_out, idx + 1)
                    vector.wait_ge(*LD[f"xres_{s}_{t}"])
                    if idx >= 2:
                        vector.wait_ge(sm.s_go0 if idx % 2 == 0 else sm.s_go1,
                                       16 * (idx // 2))
                    vector.tensor_add(out_sb[idx % 2][:], AB[idx % 2][:, 0:512],
                                      xres[idx % 2][:]).then_inc(sm.v_oev, 1)

    return nc


def _build_FT(n):
    t = np.arange(n)[:, None].astype(np.float64)
    F = np.zeros((n, n), dtype=np.float64)
    F[:, 0] = 1.0 / np.sqrt(n)
    maxf = (n - 1) // 2
    f = np.arange(1, maxf + 1)[None, :].astype(np.float64)
    ang = 2 * np.pi * f * t / n
    F[:, 1:1 + 2 * maxf:2] = np.sqrt(2.0 / n) * np.cos(ang)
    F[:, 2:2 + 2 * maxf:2] = np.sqrt(2.0 / n) * np.sin(ang)
    if n % 2 == 0:
        F[:, n - 1] = ((-1.0) ** t[:, 0]) / np.sqrt(n)
    return np.ascontiguousarray(F.T).astype(np.float32).astype(ml_dtypes.bfloat16)


def kernel(L0_attn_hidden_size, L0_attn_Wq_c, L0_attn_bq_c, L0_attn_Wq_up,
           L0_attn_bq_up, L0_attn_Wkv_c, L0_attn_bkv_c, L0_attn_Wkv_up,
           L0_attn_bkv_up, L0_attn_Wo, L0_attn_bo, L0_ffn_W_base,
           L0_ffn_b_base, L0_ffn_enc0, L0_ffn_enc1, L0_ffn_coefs0,
           L0_ffn_coefs1, L0_ffn_mask0, L0_ffn_mask1):
    f = np.float32
    x = np.asarray(L0_attn_hidden_size, dtype=f)
    Wq_c = np.asarray(L0_attn_Wq_c, dtype=f)
    bq_c = np.asarray(L0_attn_bq_c, dtype=f)
    Wq_up = np.asarray(L0_attn_Wq_up, dtype=f)
    bq_up = np.asarray(L0_attn_bq_up, dtype=f)
    Wkv_c = np.asarray(L0_attn_Wkv_c, dtype=f)
    bkv_c = np.asarray(L0_attn_bkv_c, dtype=f)
    Wkv_up = np.asarray(L0_attn_Wkv_up, dtype=f)
    bkv_up = np.asarray(L0_attn_bkv_up, dtype=f)
    Wo = np.asarray(L0_attn_Wo, dtype=f)
    bo = np.asarray(L0_attn_bo, dtype=f)
    W_base = np.asarray(L0_ffn_W_base, dtype=f)
    b_base = np.asarray(L0_ffn_b_base, dtype=f)
    enc0 = np.asarray(L0_ffn_enc0, dtype=f)
    enc1 = np.asarray(L0_ffn_enc1, dtype=f)
    coefs0 = np.asarray(L0_ffn_coefs0, dtype=f)
    coefs1 = np.asarray(L0_ffn_coefs1, dtype=f)
    mask0 = np.asarray(L0_ffn_mask0)
    mask1 = np.asarray(L0_ffn_mask1)

    perm_pairs = []
    for p in range(NP):
        a, b_ = 2 * p, 2 * p + 1
        perm_pairs += list(range(192 * a, 192 * a + 128))
        perm_pairs += list(range(192 * b_, 192 * b_ + 128))
        perm_pairs += list(range(192 * a + 128, 192 * a + 192))
        perm_pairs += list(range(192 * b_ + 128, 192 * b_ + 192))
    perm_pairs = np.array(perm_pairs)
    perm_n = np.concatenate([np.arange(256 * h, 256 * h + 128)
                             for h in range(NH)])
    perm_v = np.concatenate([np.arange(256 * h + 128, 256 * h + 256)
                             for h in range(NH)])

    common = {
        "Wqc_sl": np.ascontiguousarray(
            Wq_c.reshape(12, 128, H).transpose(0, 2, 1)),
        "bqc": np.ascontiguousarray(bq_c.reshape(12, 128).T),
        "WqupT": np.ascontiguousarray(Wq_up[perm_pairs].T),
        "bqup": np.ascontiguousarray(bq_up[perm_pairs].reshape(48, 128).T),
        "WkvcT": np.ascontiguousarray(Wkv_c.T),
        "bkvc": np.ascontiguousarray(np.pad(bkv_c, (0, 64)).reshape(5, 128).T),
        "WkunT": np.ascontiguousarray(Wkv_up[perm_n].T),
        "bkun": np.ascontiguousarray(bkv_up[perm_n].reshape(32, 128).T),
        "WkuvT": np.ascontiguousarray(Wkv_up[perm_v].T),
        "bkuv": np.ascontiguousarray(bkv_up[perm_v][None, :]),
        "WoT": np.ascontiguousarray(Wo.T),
        "bo_r": np.ascontiguousarray(bo[None, :]),
        "WbT": np.ascontiguousarray(W_base.T),
        "bb_r": np.ascontiguousarray(b_base[None, :]),
        "encT": np.ascontiguousarray(np.concatenate([enc0, enc1], 0).T),
        "co0": np.ascontiguousarray(
            coefs0.reshape(3, 32, 128, 32).transpose(0, 2, 1, 3)
            .reshape(3, 128, 1024)),
        "co1": np.ascontiguousarray(
            coefs1.reshape(3, 32, 128, 8).transpose(0, 2, 1, 3)
            .reshape(3, 128, 256)),
        "mk0": np.ascontiguousarray(
            mask0.reshape(3, 32, 128, 32).transpose(0, 2, 1, 3)
            .reshape(3, 128, 1024)).view(np.uint8),
        "mk1": np.ascontiguousarray(
            mask1.reshape(3, 32, 128, 8).transpose(0, 2, 1, 3)
            .reshape(3, 128, 256)).view(np.uint8),
        "FTb": _build_FT(H),
        "identf": np.eye(128, dtype=f),
    }

    in_maps = []
    for core in range(8):
        b_i, q_i = core // 4, core % 4
        xr = np.ascontiguousarray(np.roll(x[b_i], -TM * q_i, axis=0))
        m = dict(common)
        m["x_all"] = xr
        m["xTsp"] = np.ascontiguousarray(
            xr.T.reshape(H, 32, 128).transpose(1, 0, 2))
        in_maps.append(m)

    nc = build_nc()
    res = run_bass_kernel_spmd(nc, in_maps, core_ids=list(range(8)))

    out = np.empty((B, S, H), dtype=f)
    for core in range(8):
        b_i, q_i = core // 4, core % 4
        out[b_i, TM * q_i:TM * (q_i + 1), :] = res.results[core]["y"]
    return out


# version 13
# speedup vs baseline: 1.3026x; 1.3026x over previous
import math
import numpy as np
import ml_dtypes

import concourse.bass as bass
import concourse.mybir as mybir
from concourse.bass_utils import run_bass_kernel_spmd

F32 = mybir.dt.float32
BF16 = mybir.dt.bfloat16
U8 = mybir.dt.uint8
AF = mybir.ActivationFunctionType
ALU = mybir.AluOpType
AXX = mybir.AxisListType.X

B, S, H = 2, 4096, 4096
NH, QL, KVL, NOPE, ROPE, VH = 32, 1536, 512, 128, 64, 128
QH = NOPE + ROPE
TM = 1024
NP = 16
ISCALE = 1.0 / math.sqrt(QH)
FL_TH = [3, 3, 3, 6, 6, 6, 8, 8]
O_COL = [(0, 0), (0, 129), (0, 258), (0, 512), (0, 641), (0, 770),
         (1, 0), (1, 129)]
KV_REG4 = [(0, 0), (0, 512), (1, 0), (1, 512)]


class _Alloc:

    def __init__(self, nc):
        self.nc = nc
        self.scopes = []

    def open(self):
        self.scopes.append([])

    def close(self):
        for g in reversed(self.scopes.pop()):
            g.__exit__(None, None, None)

    def sb(self, name, shape, dt):
        g = self.nc.sbuf_tensor(name, list(shape), dt)
        t = g.__enter__()
        self.scopes[-1].append(g)
        return t


def build_nc():
    nc = bass.Bass("TRN2", target_bir_lowering=False, debug=False)

    d = lambda n, sh, dt: nc.dram_tensor(n, list(sh), dt, kind="ExternalInput").ap()
    x_all = d("x_all", (S, H), F32)
    xTsp = d("xTsp", (32, 128, 32, 128), F32)
    Wqc_sl = d("Wqc_sl", (12, 128, 4096), F32)
    bqc = d("bqc", (128, 12), F32)
    WqupT = d("WqupT", (QL, 6144), F32)
    bqup = d("bqup", (128, 48), F32)
    WkvcT = d("WkvcT", (H, 576), F32)
    bkvc = d("bkvc", (128, 5), F32)
    WkunT = d("WkunT", (16, 128, 4, 256), F32)
    bkun = d("bkun", (128, 32), F32)
    WkuvT = d("WkuvT", (16, 128, 4, 256), F32)
    bkuv = d("bkuv", (4, 1024), F32)
    WoT = d("WoT", (128, H), F32)
    bo_r = d("bo_r", (1, H), F32)
    WbT = d("WbT", (H, H), F32)
    bb_r = d("bb_r", (1, H), F32)
    encT = d("encT", (128, 1280), F32)
    co0 = d("co0", (3, 128, 1024), F32)
    co1 = d("co1", (3, 128, 256), F32)
    mk0 = d("mk0", (3, 128, 1024), U8)
    mk1 = d("mk1", (3, 128, 256), U8)
    FTb = d("FTb", (H, H), BF16)
    identf = d("identf", (128, 128), F32)
    y = nc.dram_tensor("y", [TM, H], F32, kind="ExternalOutput").ap()
    xnsp = nc.dram_tensor("xnsp", [128, 32 * 1024], BF16, kind="Internal").ap()

    al = _Alloc(nc)

    al.open()
    eT = al.sb("eT", [40, 1024], BF16)
    G = al.sb("G", [40, 4096], BF16)
    osumT = al.sb("osumT", [128, 1024], BF16)
    WoT_b = al.sb("WoT_b", [128, 4096], BF16)
    ones_rf = al.sb("ones_rf", [1, 128], F32)
    ones_rb = al.sb("ones_rb", [1, 128], BF16)
    ones_cb = al.sb("ones_cb", [128, 1], BF16)
    epsc = al.sb("epsc", [128, 1], F32)

    al.open()
    q_cT = al.sb("q_cT", [128, 12 * 1024], BF16)
    kv_nT = al.sb("kv_nT", [128, 4 * 4096], BF16)
    bkuv_b = al.sb("bkuv_b", [1, 4096], BF16)
    qrow_sp = [al.sb(f"qrow_sp{i}", [1, 128], F32) for i in range(2)]
    msq_t = al.sb("msq_t", [1, 128], F32)
    sd_t = al.sb("sd_t", [1, 128], F32)
    nrt_t = al.sb("nrt_t", [1, 128], F32)
    mb_sp = [al.sb(f"mb_sp{i}", [128, 128], BF16) for i in range(2)]
    nr_sp = [al.sb(f"nr_sp{i}", [128, 128], BF16) for i in range(2)]
    tmp2 = al.sb("tmp2", [128, 128], F32)
    m8_row = al.sb("m8_row", [1, 1024], BF16)
    nr8_row = al.sb("nr8_row", [1, 1024], BF16)
    bqc_sb = al.sb("bqc_sb", [128, 12], F32)
    bqup_sb = al.sb("bqup_sb", [128, 48], F32)
    bkvc_sb = al.sb("bkvc_sb", [128, 5], F32)
    bkun_sb = al.sb("bkun_sb", [128, 32], F32)
    S_col = al.sb("S_col", [128, 5], F32)
    mrow_sp = [al.sb(f"mrow_sp{i}", [1, 128], F32) for i in range(2)]
    nrrow_sp = [al.sb(f"nrrow_sp{i}", [1, 128], F32) for i in range(2)]
    xt2_st = [al.sb(f"xt2_st{i}", [128, 1024], F32) for i in range(2)]
    ropeT2 = al.sb("ropeT2", [128, 4096], BF16)
    o_sum = al.sb("o_sum", [128, 1024], F32)
    identf_sb = al.sb("identf_sb", [128, 128], F32)
    tmp1 = al.sb("tmp1", [128, 128], F32)
    rc_col = al.sb("rc_col", [128, 1], F32)

    al.open()
    Wkvc_bf = al.sb("Wkvc_bf", [128, 32 * 576], BF16)
    kvw_st = [al.sb(f"kvw_st{i}", [128, 576], F32) for i in range(2)]

    al.open()
    FTbuf = [al.sb(f"FTbuf{i}", [128, 4096], BF16) for i in range(2)]
    costg = al.sb("costg", [128, 1024], F32)
    mstg = al.sb("mstg", [128, 1024], U8)
    mstgf = al.sb("mstgf", [128, 1024], F32)
    A0acc = al.sb("A0acc", [128, 1024], F32)
    A1acc = al.sb("A1acc", [128, 256], F32)
    A_bf = al.sb("A_bf", [128, 32 * 40], BF16)
    al.close()

    al.open()
    xt_st = [al.sb(f"xt_st{i}", [128, 1024], F32) for i in range(3)]
    xt_bf = [al.sb(f"xt_bf{i}", [128, 32 * 128], BF16) for i in range(3)]
    sq_sp = al.sb("sq_sp", [128, 4096], BF16)
    al.close()
    al.close()

    al.open()
    xnT = al.sb("xnT", [128, 32 * 1024], BF16)
    qw_st = [al.sb(f"qw_st{i}", [128, 1024], F32) for i in range(2)]
    al.open()
    qw_bf = [al.sb(f"qw_bf{i}", [128, 32 * 128], BF16) for i in range(2)]
    al.close()
    al.open()
    encT_b = al.sb("encT_b", [128, 32 * 40], BF16)
    al.close()
    al.close()

    al.open()
    qT_n1 = al.sb("qT_n1", [128, 1024], BF16)
    qT_n2 = al.sb("qT_n2", [128, 1024], BF16)
    qT_rp = al.sb("qT_rp", [128, 1024], BF16)
    qup_st = [al.sb(f"qup_st{i}", [128, 384], F32) for i in range(3)]
    qup_bf = [al.sb(f"qup_bf{i}", [128, 12 * 384], BF16) for i in range(2)]
    kT_buf = al.sb("kT_buf", [128, 4096], BF16)
    vbuf = al.sb("vbuf", [128, 32 * 258], BF16)
    kuw_st = al.sb("kuw_st", [128, 2048], F32)
    kuw_bf = al.sb("kuw_bf", [128, 2048], BF16)
    pT = [al.sb(f"pT{i}", [128, 1024], BF16) for i in range(2)]
    vbias = al.sb("vbias", [128, 4096], F32)
    al.close()

    al.close()

    al.open()
    wb_bf = [al.sb(f"wb_bf{i}", [128, 32 * 512], BF16) for i in range(2)]
    wb_st = [al.sb(f"wb_st{i}", [128, 512], F32) for i in range(2)]
    xnT2 = al.sb("xnT2", [128, 32 * 1024], BF16)
    xres = [al.sb(f"xres{i}", [128, 512], F32) for i in range(2)]
    out_sb = [al.sb(f"out_sb{i}", [128, 512], F32) for i in range(2)]
    bias_f = al.sb("bias_f", [1, 4096], F32)
    bias_f2 = al.sb("bias_f2", [1, 4096], F32)
    bias_cb = al.sb("bias_cb", [1, 4096], BF16)
    al.close()

    al.close()

    from contextlib import ExitStack
    with ExitStack() as ctx:
        E = ctx.enter_context
        psA = E(nc.psum_tensor("psA", [128, 1024], F32))
        psB = E(nc.psum_tensor("psB", [128, 1024], F32))
        psC = E(nc.psum_tensor("psC", [128, 1024], F32))
        psD = E(nc.psum_tensor("psD", [128, 1024], F32))
        AB = [psA, psB]
        CD = [psC, psD]
        GREG = [psA, psA, psB, psB, psC, psC, psD, psD]

        sems = {}
        for name in ("s_const", "s_coef", "s_ft0", "s_ft1", "s_bkv", "v_misc",
                     "s_kvw0", "s_kvw1", "s_xt0", "s_xt1", "s_xt2", "s_xt3", "s_xt2a", "s_xt2b", "s_qw", "s_qw1",
                     "s_enc", "s_spill", "s_rope", "s_qup0", "s_qup1", "s_qup2",
                     "s_kuw", "s_xnld0", "s_bias", "s_wot0", "s_wot1",
                     "s_wb0", "s_wb1", "s_xr0", "s_xr1", "s_go0", "s_go1",
                     "ve", "t_ftu", "s_gev", "v_abf", "v_init",
                     "s_sq", "t_st1", "s_stA", "v_stB", "s_stC",
                     "v_stD", "t_bc", "v_bc", "ct_kw",
                     "t_s", "v_sev", "v_xt", "t_kv", "v_kvev", "ct_qw",
                     "t_qc", "v_qev", "ct_enc", "t_e", "v_eev", "ct_qup",
                     "ct_kuw", "t_bld", "v_bld", "t_sc", "s_exp", "t_pv",
                     "v_fl", "t_tr", "v_osT", "v_grow", "ct_wb", "t_out",
                     "v_oev", "ct_wot", "t_bk", "v_bk"):
            sems[name] = E(nc.semaphore(name))
        sm = type("SM", (), sems)

        LD = {}
        block = E(nc.Block())

        @block.sync
        def _(sync):
            cnt = {}

            def dma(name, out, in_, sem):
                sync.dma_start(out, in_).then_inc(sem, 16)
                cnt[sem] = cnt.get(sem, 0) + 16
                LD[name] = (sem, cnt[sem])

            dma("identf", identf_sb[:], identf, sm.s_const)
            dma("bqc", bqc_sb[:], bqc, sm.s_const)
            dma("bqup", bqup_sb[:], bqup, sm.s_const)
            dma("bkvc", bkvc_sb[:], bkvc, sm.s_const)
            dma("bkun", bkun_sb[:], bkun, sm.s_const)
            for e in range(3):
                if e >= 1:
                    sync.wait_ge(sm.v_init, 1 + e)
                dma(f"co0_{e}", costg[:], co0[e], sm.s_coef)
                dma(f"mk0_{e}", mstg[:], mk0[e], sm.s_coef)
            for e in range(3):
                sync.wait_ge(sm.v_init, 4 + e)
                dma(f"co1_{e}", costg[:, 0:256], co1[e], sm.s_coef)
                dma(f"mk1_{e}", mstg[:, 0:256], mk1[e], sm.s_coef)
            for k in range(32):
                if k >= 2:
                    sync.wait_ge(sm.ct_kw, k - 1)
                dma(f"kvw{k}", kvw_st[k % 2][:],
                    WkvcT[128 * k:128 * (k + 1), :],
                    sm.s_kvw0 if k % 2 == 0 else sm.s_kvw1)
            sync.wait_ge(sm.v_abf, 1)
            sync.wait_ge(sm.t_ftu, 32)
            for s in range(32):
                for g in range(4):
                    gi = 4 * s + g
                    if gi >= 3:
                        sync.wait_ge(sm.v_xt, gi - 2)
                    dma(f"xt_{s}_{g}",
                        xt_st[gi % 3][:].rearrange("p (c t) -> p c t", c=8),
                        xTsp[s][:, 8 * g:8 * (g + 1), :],
                        [sm.s_xt0, sm.s_xt1, sm.s_xt2][gi % 3])
            for s in range(8):
                for g in range(4):
                    gi2 = 4 * s + g
                    if gi2 >= 2:
                        sync.wait_ge(sm.v_xt, 128 + gi2 - 1)
                    dma(f"xt2_{s}_{g}",
                        xt2_st[gi2 % 2][:].rearrange("p (c t) -> p c t", c=8),
                        xTsp[s][:, 8 * g:8 * (g + 1), :],
                        sm.s_xt2a if gi2 % 2 == 0 else sm.s_xt2b)
            sync.wait_ge(sm.t_e, 2)
            for i in range(8):
                dma(f"spill{i}", xnsp[:, 4096 * i:4096 * (i + 1)],
                    xnT[:, 4096 * i:4096 * (i + 1)], sm.s_spill)
            sync.wait_ge(sm.v_kvev, 160)
            dma("rope_dup", ropeT2[64:128, :], ropeT2[0:64, :], sm.s_rope)
            sync.wait_ge(sm.s_spill, 128)
            for p in range(NP):
                for c in range(12):
                    gq = 12 * p + c
                    if gq >= 3:
                        sync.wait_ge(sm.ct_qup, gq - 2)
                    if p >= 2 and c == 0:
                        sync.wait_ge(sm.t_bld, 30 * (p - 1))
                    dma(f"qup_{p}_{c}", qup_st[gq % 3][:],
                        WqupT[128 * c:128 * (c + 1), 384 * p:384 * (p + 1)],
                        [sm.s_qup0, sm.s_qup1, sm.s_qup2][gq % 3])
                if p >= 1:
                    sync.wait_ge(sm.ct_kuw, p)
                dma(f"kun_{p}",
                    kuw_st[:, 0:1024].rearrange("p (c t) -> p c t", c=4),
                    WkunT[p], sm.s_kuw)
                dma(f"kuv_{p}",
                    kuw_st[:, 1024:2048].rearrange("p (c t) -> p c t", c=4),
                    WkuvT[p], sm.s_kuw)
            sync.wait_ge(sm.v_osT, 8)
            for i in range(8):
                dma(f"xnld{i}", xnT2[:, 4096 * i:4096 * (i + 1)],
                    xnsp[:, 4096 * i:4096 * (i + 1)], sm.s_xnld0)
            dma("bo", bias_f[:], bo_r, sm.s_bias)
            dma("bb", bias_f2[:], bb_r, sm.s_bias)
            for i in range(8):
                if i >= 2:
                    sync.wait_ge(sm.ct_wot, i - 1)
                dma(f"wot{i}", wb_st[i % 2][:],
                    WoT[:, 512 * i:512 * (i + 1)],
                    sm.s_wot0 if i % 2 == 0 else sm.s_wot1)
            sync.wait_ge(sm.ct_wot, 8)
            for k in range(32):
                if k >= 2:
                    sync.wait_ge(sm.ct_wb, k - 1)
                dma(f"wb_0_{k}", wb_st[k % 2][:],
                    WbT[128 * k:128 * (k + 1), 0:512],
                    sm.s_wb0 if k % 2 == 0 else sm.s_wb1)
            for s in range(8):
                for t in range(8):
                    idx = 8 * s + t
                    if idx >= 2:
                        sync.wait_ge(sm.v_oev, idx - 1)
                    dma(f"xres_{s}_{t}", xres[idx % 2][:],
                        x_all[128 * t:128 * (t + 1), 512 * s:512 * (s + 1)],
                        sm.s_xr0 if idx % 2 == 0 else sm.s_xr1)
                    if s + 1 < 8:
                        for k in range(4 * t, 4 * t + 4):
                            ck = 32 * (s + 1) + k
                            sync.wait_ge(sm.ct_wb, ck - 1)
                            dma(f"wb_{s+1}_{k}", wb_st[k % 2][:],
                                WbT[128 * k:128 * (k + 1),
                                    512 * (s + 1):512 * (s + 2)],
                                sm.s_wb0 if k % 2 == 0 else sm.s_wb1)

        @block.gpsimd
        def _(gp):
            for idx in range(64):
                s, t = idx // 8, idx % 8
                gp.wait_ge(sm.v_oev, idx + 1)
                gp.dma_start(y[128 * t:128 * (t + 1), 512 * s:512 * (s + 1)],
                             out_sb[idx % 2][:]).then_inc(
                    sm.s_go0 if idx % 2 == 0 else sm.s_go1, 16)
            gp.wait_ge(sm.s_go0, 16 * 32)
            gp.wait_ge(sm.s_go1, 16 * 32)

        @block.tensor
        def _(tensor):
            tensor.wait_ge(sm.v_abf, 1)
            for k in range(32):
                tensor.wait_ge(sm.s_ft0 if k % 2 == 0 else sm.s_ft1,
                               16 * (k // 2 + 1))
                for sp in range(8):
                    reg = GREG[sp]
                    col = (sp % 2) * 512
                    mm = tensor.matmul(reg[0:40, col:col + 512],
                                       A_bf[:, 40 * k:40 * (k + 1)],
                                       FTbuf[k % 2][:, 512 * sp:512 * (sp + 1)],
                                       start=(k == 0), stop=(k == 31))
                mm.then_inc(sm.t_ftu, 1)
            tensor.wait_ge(sm.s_gev, 8)
            tensor.wait_ge(sm.ct_kw, 32)
            tensor.wait_ge(sm.v_init, 1)
            for m in range(5):
                msz = 128 if m < 4 else 64
                if m >= 2:
                    tensor.wait_ge(sm.v_sev, m - 1)
                for k in range(32):
                    mm = tensor.matmul(
                        CD[m % 2][0:msz, 512:513],
                        Wkvc_bf[:, 576 * k + 128 * m:576 * k + 128 * m + msz],
                        ones_cb[:], start=(k == 0), stop=(k == 31))
                mm.then_inc(sm.t_s, 1)
            tensor.wait_ge(sm.s_const, 80)
            tensor.wait_ge(sm.v_sev, 5)

            def emit_stats(sp2):
                tensor.wait_ge(sm.v_xt, 4 * (sp2 + 1))
                if sp2 >= 2:
                    tensor.wait_ge(sm.s_stA, sp2 - 1)
                reg = CD[sp2 % 2]
                for k in range(32):
                    mm = tensor.matmul(
                        reg[0:1, 512:640], ones_cb[:],
                        xt_bf[sp2 % 3][:, 128 * k:128 * (k + 1)],
                        start=(k == 0), stop=(k == 31))
                tensor.wait_ge(sm.s_sq, sp2 + 1)
                for k in range(32):
                    mm = tensor.matmul(
                        reg[0:1, 640:768], ones_cb[:],
                        sq_sp[:, 128 * k:128 * (k + 1)],
                        start=(k == 0), stop=(k == 31))
                mm.then_inc(sm.t_st1, 1)

            def emit_kv(s, ms):
                for m in ms:
                    i = 5 * s + m
                    msz = 128 if m < 4 else 64
                    kreg, kcol = KV_REG4[i % 4]
                    if i >= 4:
                        tensor.wait_ge(sm.v_kvev, i - 3)
                    for k in range(32):
                        mm = tensor.matmul(
                            AB[kreg][0:msz, kcol:kcol + 128],
                            Wkvc_bf[:, 576 * k + 128 * m:576 * k + 128 * m + msz],
                            xt_bf[s % 3][:, 128 * k:128 * (k + 1)],
                            start=(k == 0), stop=(k == 31))
                    mm.then_inc(sm.t_kv, 1)

            emit_stats(0)
            for s in range(32):
                tensor.wait_ge(sm.v_xt, 4 * (s + 1))
                emit_kv(s, range(4))
                if s + 1 < 32:
                    emit_stats(s + 1)
                tensor.wait_ge(sm.v_stD, s + 1)
                if 2 * s >= 2:
                    tensor.wait_ge(sm.v_bc, 2 * s - 1)
                tensor.matmul(CD[0][0:128, 0:128], ones_rf[:],
                              mrow_sp[s % 2][:],
                              start=True, stop=True).then_inc(sm.t_bc, 1)
                if 2 * s + 1 >= 2:
                    tensor.wait_ge(sm.v_bc, 2 * s)
                tensor.matmul(CD[1][0:128, 0:128], ones_rf[:],
                              nrrow_sp[s % 2][:],
                              start=True, stop=True).then_inc(sm.t_bc, 1)
                emit_kv(s, [4])
            for s in range(8):
                for w in range(2):
                    bi = 64 + 2 * s + w
                    tensor.wait_ge(sm.v_bc, bi - 1)
                    row = m8_row if w == 0 else nr8_row
                    tensor.matmul(CD[w][0:128, 0:128], ones_rb[:],
                                  row[0:1, 128 * s:128 * (s + 1)],
                                  start=True, stop=True).then_inc(sm.t_bc, 1)
            for l in range(12):
                tensor.wait_ge(sm.ct_qw, 4 * (l + 1))
                for sp in range(2):
                    i = 2 * l + sp
                    tensor.wait_ge(sm.v_xt, 128 + 16 * (sp + 1))
                    if i >= 2:
                        tensor.wait_ge(sm.v_qev, i - 1)
                    for k in range(32):
                        mm = tensor.matmul(
                            AB[i % 2][:, 0:512],
                            qw_bf[l % 2][:, 128 * k:128 * (k + 1)],
                            xnT[:, 1024 * k + 512 * sp:1024 * k + 512 * sp + 512],
                            start=(k == 0), stop=(k == 31))
                    mm.then_inc(sm.t_qc, 1)
            tensor.wait_ge(sm.ct_enc, 1)
            tensor.wait_ge(sm.v_bc, 80)
            for sp in range(2):
                for k in range(32):
                    mm = tensor.matmul(
                        CD[sp][0:40, 0:512],
                        encT_b[:, 40 * k:40 * (k + 1)],
                        xnT[:, 1024 * k + 512 * sp:1024 * k + 512 * sp + 512],
                        start=(k == 0), stop=(k == 31))
                mm.then_inc(sm.t_e, 1)
            tensor.wait_ge(sm.v_qev, 24)
            tensor.wait_ge(sm.v_eev, 2)
            tensor.wait_ge(*LD["rope_dup"])
            tensor.wait_ge(sm.v_misc, 4)
            VBREG = [psC[:, 0:512], psC[:, 512:1024],
                     psD[:, 0:512], psD[:, 512:1024]]
            for i in range(8):
                if i >= 4:
                    tensor.wait_ge(sm.v_bk, i - 3)
                tensor.matmul(VBREG[i % 4], ones_rb[:],
                              bkuv_b[0:1, 512 * i:512 * (i + 1)],
                              start=True, stop=True).then_inc(sm.t_bk, 1)
            tensor.wait_ge(sm.v_bk, 8)
            for p in range(NP):
                if p > 0:
                    tensor.wait_ge(sm.ct_qup, 12 * (p + 1))
                for gq in range(6):
                    bi = 30 * p + gq
                    if gq == 0 and p > 0:
                        tensor.wait_ge(sm.s_exp, 64 * p - 1)
                    if gq == 1 and p > 0:
                        tensor.wait_ge(sm.s_exp, 64 * p)
                    if gq == 1 and p == 0:
                        tensor.wait_ge(sm.ct_qup, 12)
                    if bi >= 2:
                        tensor.wait_ge(sm.v_bld, bi - 1)
                    mch, sp = gq // 2, gq % 2
                    for k in range(12):
                        if p == 0 and gq == 0:
                            tensor.wait_ge(sm.ct_qup, k + 1)
                        mm = tensor.matmul(
                            AB[bi % 2][:, 0:512],
                            qup_bf[p % 2][:, 384 * k + 128 * mch:
                                          384 * k + 128 * mch + 128],
                            q_cT[:, 1024 * k + 512 * sp:1024 * k + 512 * sp + 512],
                            start=(k == 0), stop=(k == 11))
                    mm.then_inc(sm.t_bld, 1)
                tensor.wait_ge(sm.ct_kuw, p + 1)
                for r in range(8):
                    bi = 30 * p + 6 + r
                    if bi >= 2:
                        tensor.wait_ge(sm.v_bld, bi - 1)
                    for wl in range(4):
                        w = 4 * r + wl
                        col = 256 * wl
                        for rc in range(4):
                            mm = tensor.matmul(
                                AB[bi % 2][:, col:col + 256],
                                kv_nT[:, 4096 * rc + 128 * w:
                                      4096 * rc + 128 * w + 128],
                                kuw_bf[:, 1024 + 256 * rc:1024 + 256 * (rc + 1)],
                                start=(rc == 0), stop=(rc == 3))
                    mm.then_inc(sm.t_bld, 1)
                for hh in range(2):
                    h = 2 * p + hh
                    for sp in range(8):
                        bi = 30 * p + 14 + 8 * hh + sp
                        if sp == 0 and hh == 1:
                            tensor.wait_ge(sm.s_exp, 32 * h - 1)
                        if sp == 1 and hh == 1:
                            tensor.wait_ge(sm.s_exp, 32 * h)
                        if bi >= 2:
                            tensor.wait_ge(sm.v_bld, bi - 1)
                        for rc in range(4):
                            mm = tensor.matmul(
                                AB[bi % 2][:, 0:512],
                                kuw_bf[:, 256 * rc + 128 * hh:
                                       256 * rc + 128 * hh + 128],
                                kv_nT[:, 4096 * rc + 512 * sp:
                                      4096 * rc + 512 * sp + 512],
                                start=(rc == 0), stop=(rc == 3))
                        mm.then_inc(sm.t_bld, 1)
                    qn = qT_n1 if hh == 0 else qT_n2
                    rbase = 64 * hh

                    def emit_scores(kwin):
                        idx = 32 * h + kwin
                        if kwin < 2:
                            tensor.wait_ge(sm.v_bld, 30 * p + 14 + 8 * (hh + 1))
                        else:
                            tensor.wait_ge(sm.s_exp, idx - 1)
                        reg = AB[kwin % 2]
                        for sp in range(2):
                            tensor.matmul(reg[:, 512 * sp:512 * (sp + 1)],
                                          kT_buf[:, 128 * kwin:128 * (kwin + 1)],
                                          qn[:, 512 * sp:512 * (sp + 1)],
                                          start=True, stop=False)
                            mm = tensor.matmul(
                                reg[:, 512 * sp:512 * (sp + 1)],
                                ropeT2[rbase:rbase + 64,
                                       128 * kwin:128 * (kwin + 1)],
                                qT_rp[rbase:rbase + 64, 512 * sp:512 * (sp + 1)],
                                start=False, stop=True)
                        mm.then_inc(sm.t_sc, 1)

                    def emit_pv(kwin):
                        idx = 32 * h + kwin
                        tensor.wait_ge(sm.s_exp, idx + 1)
                        if kwin == 0 and h > 0:
                            for qt in range(8):
                                tensor.wait_ge(sm.v_fl,
                                               8 * (h - 1) + FL_TH[qt])
                        for qt in range(8):
                            regi, col = O_COL[qt]
                            mm = tensor.matmul(
                                CD[regi][:, col:col + 129],
                                pT[kwin % 2][:, 128 * qt:128 * (qt + 1)],
                                vbuf[:, 258 * kwin + 129 * hh:
                                     258 * kwin + 129 * hh + 129],
                                start=(kwin == 0 and qt in (0, 3, 6)),
                                stop=(kwin == 31 and qt in (2, 5, 7)))
                        mm.then_inc(sm.t_pv, 1)

                    emit_scores(0)
                    emit_scores(1)
                    for kwin in range(32):
                        if kwin + 2 < 32:
                            emit_scores(kwin + 2)
                        emit_pv(kwin)
            tensor.wait_ge(sm.v_fl, 8 * NH)
            for qt in range(8):
                if qt >= 2:
                    tensor.wait_ge(sm.v_osT, qt - 1)
                base = 512 + 128 * (qt % 2)
                tensor.transpose(psD[:, base:base + 128],
                                 o_sum[:, 128 * qt:128 * (qt + 1)],
                                 identf_sb[:]).then_inc(sm.t_tr, 1)
            tensor.wait_ge(sm.v_osT, 8)
            tensor.wait_ge(sm.v_grow, 1)
            tensor.wait_ge(sm.s_xnld0, 128)
            for s in range(8):
                tensor.wait_ge(sm.ct_wb, 32 * (s + 1))
                for t in range(8):
                    idx = 8 * s + t
                    if idx >= 2:
                        tensor.wait_ge(sm.v_oev, idx - 1)
                    reg = AB[idx % 2]
                    for k in range(32):
                        tensor.matmul(
                            reg[:, 0:512],
                            xnT2[:, 1024 * k + 128 * t:1024 * k + 128 * t + 128],
                            wb_bf[s % 2][:, 512 * k:512 * (k + 1)],
                            start=(k == 0), stop=False)
                    tensor.matmul(reg[:, 0:512], eT[:, 128 * t:128 * (t + 1)],
                                  G[:, 512 * s:512 * (s + 1)],
                                  start=False, stop=False)
                    tensor.matmul(reg[:, 0:512], ones_rb[:],
                                  bias_cb[0:1, 512 * s:512 * (s + 1)],
                                  start=False, stop=False)
                    tensor.matmul(reg[:, 0:512], osumT[:, 128 * t:128 * (t + 1)],
                                  WoT_b[:, 512 * s:512 * (s + 1)],
                                  start=False, stop=True).then_inc(sm.t_out, 1)

        @block.scalar
        def _(scalar):
            for j in range(32):
                if j >= 2:
                    scalar.wait_ge(sm.t_ftu, j - 1)
                scalar.dma_start(FTbuf[j % 2][:],
                                 FTb[128 * j:128 * (j + 1), :]).then_inc(
                    sm.s_ft0 if j % 2 == 0 else sm.s_ft1, 16)
            scalar.wait_ge(sm.t_ftu, 32)
            for sp in range(8):
                reg = GREG[sp]
                col = (sp % 2) * 512
                scalar.activation(G[0:40, 512 * sp:512 * (sp + 1)],
                                  reg[0:40, col:col + 512], AF.Copy,
                                  scale=0.5).then_inc(sm.s_gev, 1)
            scalar.wait_ge(sm.v_xt, 4)
            scalar.activation(sq_sp[:], xt_bf[0][:],
                              AF.Square).then_inc(sm.s_sq, 1)
            for sspan in range(32):
                if sspan + 1 < 32:
                    scalar.wait_ge(sm.v_xt, 4 * (sspan + 2))
                    scalar.wait_ge(sm.t_st1, sspan + 1)
                    scalar.activation(sq_sp[:], xt_bf[(sspan + 1) % 3][:],
                                      AF.Square).then_inc(sm.s_sq, 1)
                scalar.wait_ge(sm.t_st1, sspan + 1)
                if sspan >= 2:
                    scalar.wait_ge(sm.t_bc, 2 * sspan - 2)
                    scalar.wait_ge(sm.v_stB, sspan - 1)
                reg = CD[sspan % 2]
                scalar.activation(mrow_sp[sspan % 2][:], reg[0:1, 512:640],
                                  AF.Copy, scale=1.0 / H)
                scalar.activation(qrow_sp[sspan % 2][:], reg[0:1, 640:768],
                                  AF.Copy, scale=1.0 / H).then_inc(sm.s_stA, 1)
                scalar.wait_ge(sm.v_stB, sspan + 1)
                scalar.activation(sd_t[:], qrow_sp[sspan % 2][:], AF.Sqrt,
                                  bias=epsc[0:1, :]).then_inc(sm.s_stC, 1)
            scalar.wait_ge(sm.t_kv, 160)
            for l in range(12):
                for qtr in range(4):
                    j = 4 * l + qtr
                    if j >= 2:
                        scalar.wait_ge(sm.ct_qw, j - 1)
                    scalar.dma_start(
                        qw_st[j % 2][:],
                        Wqc_sl[l][:, 1024 * qtr:1024 * (qtr + 1)]).then_inc(
                        sm.s_qw if j % 2 == 0 else sm.s_qw1, 16)
                    LD[f"qw{l}_{qtr}"] = (sm.s_qw if j % 2 == 0 else sm.s_qw1,
                                          16 * (j // 2 + 1))
            scalar.wait_ge(sm.ct_qw, 48)
            scalar.dma_start(qw_st[0][:], encT[:, 0:1024]).then_inc(sm.s_enc, 16)
            scalar.dma_start(qw_st[1][:, 0:256],
                             encT[:, 1024:1280]).then_inc(sm.s_enc, 16)
            scalar.wait_ge(sm.ct_enc, 1)
            for c in range(4):
                if c >= 1:
                    scalar.wait_ge(sm.v_misc, c)
                scalar.dma_start(qw_st[c % 2][0:1, :],
                                 bkuv[c:c + 1, :]).then_inc(sm.s_bkv, 16)
            scalar.wait_ge(sm.v_misc, 4)
            for h in range(NH):
                for kwin in range(32):
                    idx = 32 * h + kwin
                    scalar.wait_ge(sm.t_sc, idx + 1)
                    if idx >= 2:
                        scalar.wait_ge(sm.t_pv, idx - 1)
                    scalar.activation(pT[kwin % 2][:], AB[kwin % 2][:, 0:1024],
                                      AF.Exp, scale=ISCALE).then_inc(sm.s_exp, 1)

        @block.vector
        def _(vector):
            nve = [0]

            def vedge(ins):
                ins.then_inc(sm.ve, 1)
                nve[0] += 1
                vector.wait_ge(sm.ve, nve[0])

            vector.memset(ones_rf[:], 1.0)
            vector.memset(ones_rb[:], 1.0)
            vector.memset(ones_cb[:], 1.0)
            vector.memset(epsc[:], 1e-5)
            vector.memset(o_sum[:], 0.0).then_inc(sm.v_init, 1)
            vector.wait_ge(sm.s_const, 80)
            for e in range(3):
                vector.wait_ge(*LD[f"mk0_{e}"])
                vedge(vector.tensor_copy(mstgf[:], mstg[:]))
                if e == 0:
                    vedge(vector.tensor_tensor(A0acc[:], costg[:], mstgf[:],
                                               ALU.mult))
                else:
                    vedge(vector.tensor_tensor(mstgf[:], costg[:], mstgf[:],
                                               ALU.mult))
                    vedge(vector.tensor_add(A0acc[:], A0acc[:], mstgf[:]))
                vector.engine_nop().then_inc(sm.v_init, 1)
            for e in range(3):
                vector.wait_ge(*LD[f"mk1_{e}"])
                vedge(vector.tensor_copy(mstgf[:, 0:256], mstg[:, 0:256]))
                if e == 0:
                    vedge(vector.tensor_tensor(A1acc[:], costg[:, 0:256],
                                               mstgf[:, 0:256], ALU.mult))
                else:
                    vedge(vector.tensor_tensor(mstgf[:, 0:256], costg[:, 0:256],
                                               mstgf[:, 0:256], ALU.mult))
                    vedge(vector.tensor_add(A1acc[:], A1acc[:],
                                            mstgf[:, 0:256]))
                vector.engine_nop().then_inc(sm.v_init, 1)
            for c in range(32):
                vector.tensor_copy(A_bf[:, 40 * c:40 * c + 32],
                                   A0acc[:, 32 * c:32 * (c + 1)])
                vector.tensor_copy(A_bf[:, 40 * c + 32:40 * c + 40],
                                   A1acc[:, 8 * c:8 * (c + 1)])
            vector.engine_nop().then_inc(sm.v_abf, 1)
            for k in range(32):
                vector.wait_ge(*LD[f"kvw{k}"])
                vector.tensor_copy(Wkvc_bf[:, 576 * k:576 * (k + 1)],
                                   kvw_st[k % 2][:]).then_inc(sm.ct_kw, 1)
            for m in range(5):
                msz = 128 if m < 4 else 64
                vector.wait_ge(sm.t_s, m + 1)
                vector.tensor_copy(S_col[0:msz, m:m + 1],
                                   CD[m % 2][0:msz, 512:513]).then_inc(sm.v_sev, 1)
            def emit_casts(sp2):
                for g in range(4):
                    gi = 4 * sp2 + g
                    vector.wait_ge(*LD[f"xt_{sp2}_{g}"])
                    if sp2 >= 3 and g == 0:
                        vector.wait_ge(sm.t_kv, 5 * (sp2 - 2))
                    vector.tensor_copy(
                        xt_bf[sp2 % 3][:, 1024 * g:1024 * (g + 1)],
                        xt_st[gi % 3][:]).then_inc(sm.v_xt, 1)

            emit_casts(0)
            for s in range(32):
                if s + 1 < 32:
                    emit_casts(s + 1)
                vector.wait_ge(sm.s_stA, s + 1)
                vedge(vector.tensor_tensor(msq_t[:], mrow_sp[s % 2][:],
                                           mrow_sp[s % 2][:], ALU.mult))
                vector.tensor_sub(qrow_sp[s % 2][:], qrow_sp[s % 2][:],
                                  msq_t[:]).then_inc(sm.v_stB, 1)
                vector.wait_ge(sm.s_stC, s + 1)
                vedge(vector.reciprocal(nrt_t[:], sd_t[:]))
                vedge(vector.tensor_scalar_mul(nrrow_sp[s % 2][:], nrt_t[:],
                                               -1.0))
                if s < 8:
                    vector.tensor_copy(m8_row[0:1, 128 * s:128 * (s + 1)],
                                       mrow_sp[s % 2][:])
                    vector.tensor_copy(nr8_row[0:1, 128 * s:128 * (s + 1)],
                                       nrrow_sp[s % 2][:])
                vector.engine_nop().then_inc(sm.v_stD, 1)
                for w in range(2):
                    bi = 2 * s + w
                    vector.wait_ge(sm.t_bc, bi + 1)
                    dst = mb_sp[s % 2] if w == 0 else nr_sp[s % 2]
                    vedge(vector.tensor_copy(dst[:], CD[w][0:128, 0:128]))
                    vector.engine_nop().then_inc(sm.v_bc, 1)
                for m in range(5):
                    i = 5 * s + m
                    msz = 128 if m < 4 else 64
                    kreg, kcol = KV_REG4[i % 4]
                    vector.wait_ge(sm.t_kv, i + 1)
                    vedge(vector.scalar_tensor_tensor(
                        tmp1[0:msz, :], mb_sp[s % 2][0:msz, :],
                        S_col[0:msz, m:m + 1],
                        AB[kreg][0:msz, kcol:kcol + 128],
                        ALU.mult, ALU.subtract))
                    vedge(vector.tensor_mul(tmp2[0:msz, :], tmp1[0:msz, :],
                                            nr_sp[s % 2][0:msz, :]))
                    if m < 4:
                        dst = kv_nT[:, 4096 * m + 128 * s:4096 * m + 128 * s + 128]
                        vector.tensor_scalar(dst, tmp2[:],
                                             bkvc_sb[:, m:m + 1],
                                             None, ALU.add).then_inc(sm.v_kvev, 1)
                    else:
                        dst = ropeT2[0:64, 128 * s:128 * (s + 1)]
                        vector.tensor_scalar(dst, tmp2[0:64, :],
                                             bkvc_sb[0:64, m:m + 1], None,
                                             ALU.add).then_inc(sm.v_kvev, 1)
            def qw_cast(l):
                for qtr in range(4):
                    j = 4 * l + qtr
                    vector.wait_ge(*LD[f"qw{l}_{qtr}"])
                    if l >= 2 and qtr == 0:
                        vector.wait_ge(sm.t_qc, 2 * l - 2)
                    vector.tensor_copy(
                        qw_bf[l % 2][:, 1024 * qtr:1024 * (qtr + 1)],
                        qw_st[j % 2][:]).then_inc(sm.ct_qw, 1)

            def qc_evac(l):
                for sp in range(2):
                    i = 2 * l + sp
                    vector.wait_ge(sm.t_qc, i + 1)
                    vector.tensor_scalar(
                        q_cT[:, 1024 * l + 512 * sp:1024 * l + 512 * sp + 512],
                        AB[i % 2][:, 0:512], bqc_sb[:, l:l + 1], None,
                        ALU.add).then_inc(sm.v_qev, 1)

            for s in range(8):
                for w in range(2):
                    bi = 64 + 2 * s + w
                    vector.wait_ge(sm.t_bc, bi + 1)
                    dst = mb_sp[s % 2] if w == 0 else nr_sp[s % 2]
                    vedge(vector.tensor_copy(dst[:], CD[w][0:128, 0:128]))
                    vector.engine_nop().then_inc(sm.v_bc, 1)
                for g in range(4):
                    gi2 = 4 * s + g
                    vector.wait_ge(*LD[f"xt2_{s}_{g}"])
                    for ci in range(8):
                        k = 8 * g + ci
                        dst = xnT[:, 1024 * k + 128 * s:1024 * k + 128 * s + 128]
                        vedge(vector.tensor_sub(
                            tmp1[:], mb_sp[s % 2][:],
                            xt2_st[gi2 % 2][:, 128 * ci:128 * (ci + 1)]))
                        vedge(vector.tensor_mul(dst, tmp1[:], nr_sp[s % 2][:]))
                    vector.engine_nop().then_inc(sm.v_xt, 1)
                if s < 2:
                    qw_cast(s)
            for l in range(2, 12):
                qw_cast(l)
                qc_evac(l - 2)
            qc_evac(10)
            qc_evac(11)
            vector.wait_ge(sm.s_enc, 32)
            vector.wait_ge(sm.t_qc, 24)
            vedge(vector.tensor_copy(encT_b[:, 0:1024], qw_st[0][:]))
            vector.tensor_copy(encT_b[:, 1024:1280],
                               qw_st[1][:, 0:256]).then_inc(sm.ct_enc, 1)
            for c in range(4):
                vector.wait_ge(sm.s_bkv, 16 * (c + 1))
                vector.tensor_copy(bkuv_b[0:1, 1024 * c:1024 * (c + 1)],
                                   qw_st[c % 2][0:1, :]).then_inc(sm.v_misc, 1)
            for sp in range(2):
                vector.wait_ge(sm.t_e, sp + 1)
                vector.tensor_copy(eT[0:40, 512 * sp:512 * (sp + 1)],
                                   CD[sp][0:40, 0:512]).then_inc(sm.v_eev, 1)
            vector.wait_ge(sm.s_spill, 128)
            vector.wait_ge(sm.t_e, 2)
            vector.memset(vbuf[:], 1.0)
            VBREG4 = [psC[:, 0:512], psC[:, 512:1024],
                      psD[:, 0:512], psD[:, 512:1024]]
            for i in range(8):
                vector.wait_ge(sm.t_bk, i + 1)
                vector.tensor_copy(vbias[:, 512 * i:512 * (i + 1)],
                                   VBREG4[i % 4]).then_inc(sm.v_bk, 1)

            def qup_cast(pp, c):
                vector.wait_ge(*LD[f"qup_{pp}_{c}"])
                if pp >= 2 and c == 0:
                    vector.wait_ge(sm.t_bld, 30 * (pp - 2) + 6)
                vector.tensor_copy(qup_bf[pp % 2][:, 384 * c:384 * (c + 1)],
                                   qup_st[(12 * pp + c) % 3][:]).then_inc(
                    sm.ct_qup, 1)

            def kuw_cast(pp):
                vector.wait_ge(*LD[f"kuv_{pp}"])
                if pp >= 1:
                    vector.wait_ge(sm.t_bld, 30 * pp)
                vector.tensor_copy(kuw_bf[:], kuw_st[:]).then_inc(sm.ct_kuw, 1)

            for c in range(12):
                qup_cast(0, c)
            kuw_cast(0)
            for p in range(NP):
                for gq in range(6):
                    bi = 30 * p + gq
                    vector.wait_ge(sm.t_bld, bi + 1)
                    mch, sp = gq // 2, gq % 2
                    dst = [qT_n1, qT_n2, qT_rp][mch]
                    vector.tensor_scalar(
                        dst[:, 512 * sp:512 * (sp + 1)], AB[bi % 2][:, 0:512],
                        bqup_sb[:, 3 * p + mch:3 * p + mch + 1], None,
                        ALU.add).then_inc(sm.v_bld, 1)
                for r in range(8):
                    bi = 30 * p + 6 + r
                    vector.wait_ge(sm.t_bld, bi + 1)
                    for wl in range(4):
                        w = 4 * r + wl
                        for hh in range(2):
                            vector.tensor_tensor(
                                vbuf[:, 258 * w + 129 * hh:
                                     258 * w + 129 * hh + 128],
                                AB[bi % 2][:, 256 * wl + 128 * hh:
                                           256 * wl + 128 * hh + 128],
                                vbias[:, 128 * (2 * p + hh):
                                      128 * (2 * p + hh) + 128],
                                ALU.add)
                    vector.engine_nop().then_inc(sm.v_bld, 1)
                for hh in range(2):
                    h = 2 * p + hh
                    for sp in range(8):
                        bi = 30 * p + 14 + 8 * hh + sp
                        vector.wait_ge(sm.t_bld, bi + 1)
                        vector.tensor_scalar(
                            kT_buf[:, 512 * sp:512 * (sp + 1)],
                            AB[bi % 2][:, 0:512], bkun_sb[:, h:h + 1], None,
                            ALU.add).then_inc(sm.v_bld, 1)
                    if hh == 0 and p + 1 < NP:
                        for c in range(12):
                            qup_cast(p + 1, c)
                    if hh == 1 and p + 1 < NP:
                        kuw_cast(p + 1)
                    vector.wait_ge(sm.t_pv, 32 * (h + 1))
                    for qt in range(8):
                        regi, col = O_COL[qt]
                        oreg = CD[regi]
                        vedge(vector.reciprocal(rc_col[:],
                                                oreg[:, col + 128:col + 129]))
                        vedge(vector.tensor_scalar(tmp1[:],
                                                   oreg[:, col:col + 128],
                                                   rc_col[:], None, ALU.mult))
                        vector.tensor_add(o_sum[:, 128 * qt:128 * (qt + 1)],
                                          o_sum[:, 128 * qt:128 * (qt + 1)],
                                          tmp1[:]).then_inc(sm.v_fl, 1)
            for qt in range(8):
                vector.wait_ge(sm.t_tr, qt + 1)
                base = 512 + 128 * (qt % 2)
                vector.tensor_copy(osumT[:, 128 * qt:128 * (qt + 1)],
                                   psD[:, base:base + 128]).then_inc(sm.v_osT, 1)
            vector.wait_ge(sm.s_bias, 32)
            vedge(vector.tensor_add(bias_f[:], bias_f[:], bias_f2[:]))
            vector.tensor_copy(bias_cb[:], bias_f[:]).then_inc(sm.v_grow, 1)
            for i in range(8):
                vector.wait_ge(*LD[f"wot{i}"])
                vector.tensor_copy(WoT_b[:, 512 * i:512 * (i + 1)],
                                   wb_st[i % 2][:]).then_inc(sm.ct_wot, 1)
            for k in range(32):
                vector.wait_ge(*LD[f"wb_0_{k}"])
                vector.tensor_copy(wb_bf[0][:, 512 * k:512 * (k + 1)],
                                   wb_st[k % 2][:]).then_inc(sm.ct_wb, 1)
            for s in range(8):
                for t in range(8):
                    idx = 8 * s + t
                    if s + 1 < 8:
                        for k in range(4 * t, 4 * t + 4):
                            ck = 32 * (s + 1) + k
                            vector.wait_ge(*LD[f"wb_{s+1}_{k}"])
                            if s + 1 >= 2 and k == 0:
                                vector.wait_ge(sm.t_out, 8 * s)
                            vector.tensor_copy(
                                wb_bf[(s + 1) % 2][:, 512 * k:512 * (k + 1)],
                                wb_st[k % 2][:]).then_inc(sm.ct_wb, 1)
                    vector.wait_ge(sm.t_out, idx + 1)
                    vector.wait_ge(*LD[f"xres_{s}_{t}"])
                    if idx >= 2:
                        vector.wait_ge(sm.s_go0 if idx % 2 == 0 else sm.s_go1,
                                       16 * (idx // 2))
                    vector.tensor_add(out_sb[idx % 2][:], AB[idx % 2][:, 0:512],
                                      xres[idx % 2][:]).then_inc(sm.v_oev, 1)

    return nc


def _build_FT(n):
    t = np.arange(n)[:, None].astype(np.float64)
    F = np.zeros((n, n), dtype=np.float64)
    F[:, 0] = 1.0 / np.sqrt(n)
    maxf = (n - 1) // 2
    f = np.arange(1, maxf + 1)[None, :].astype(np.float64)
    ang = 2 * np.pi * f * t / n
    F[:, 1:1 + 2 * maxf:2] = np.sqrt(2.0 / n) * np.cos(ang)
    F[:, 2:2 + 2 * maxf:2] = np.sqrt(2.0 / n) * np.sin(ang)
    if n % 2 == 0:
        F[:, n - 1] = ((-1.0) ** t[:, 0]) / np.sqrt(n)
    return np.ascontiguousarray(F.T).astype(np.float32).astype(ml_dtypes.bfloat16)


def kernel(L0_attn_hidden_size, L0_attn_Wq_c, L0_attn_bq_c, L0_attn_Wq_up,
           L0_attn_bq_up, L0_attn_Wkv_c, L0_attn_bkv_c, L0_attn_Wkv_up,
           L0_attn_bkv_up, L0_attn_Wo, L0_attn_bo, L0_ffn_W_base,
           L0_ffn_b_base, L0_ffn_enc0, L0_ffn_enc1, L0_ffn_coefs0,
           L0_ffn_coefs1, L0_ffn_mask0, L0_ffn_mask1):
    f = np.float32
    x = np.asarray(L0_attn_hidden_size, dtype=f)
    Wq_c = np.asarray(L0_attn_Wq_c, dtype=f)
    bq_c = np.asarray(L0_attn_bq_c, dtype=f)
    Wq_up = np.asarray(L0_attn_Wq_up, dtype=f)
    bq_up = np.asarray(L0_attn_bq_up, dtype=f)
    Wkv_c = np.asarray(L0_attn_Wkv_c, dtype=f)
    bkv_c = np.asarray(L0_attn_bkv_c, dtype=f)
    Wkv_up = np.asarray(L0_attn_Wkv_up, dtype=f)
    bkv_up = np.asarray(L0_attn_bkv_up, dtype=f)
    Wo = np.asarray(L0_attn_Wo, dtype=f)
    bo = np.asarray(L0_attn_bo, dtype=f)
    W_base = np.asarray(L0_ffn_W_base, dtype=f)
    b_base = np.asarray(L0_ffn_b_base, dtype=f)
    enc0 = np.asarray(L0_ffn_enc0, dtype=f)
    enc1 = np.asarray(L0_ffn_enc1, dtype=f)
    coefs0 = np.asarray(L0_ffn_coefs0, dtype=f)
    coefs1 = np.asarray(L0_ffn_coefs1, dtype=f)
    mask0 = np.asarray(L0_ffn_mask0)
    mask1 = np.asarray(L0_ffn_mask1)

    perm_pairs = []
    for p in range(NP):
        a, b_ = 2 * p, 2 * p + 1
        perm_pairs += list(range(192 * a, 192 * a + 128))
        perm_pairs += list(range(192 * b_, 192 * b_ + 128))
        perm_pairs += list(range(192 * a + 128, 192 * a + 192))
        perm_pairs += list(range(192 * b_ + 128, 192 * b_ + 192))
    perm_pairs = np.array(perm_pairs)
    perm_n = np.concatenate([np.arange(256 * h, 256 * h + 128)
                             for h in range(NH)])
    perm_v = np.concatenate([np.arange(256 * h + 128, 256 * h + 256)
                             for h in range(NH)])

    common = {
        "Wqc_sl": np.ascontiguousarray(
            Wq_c.reshape(12, 128, 32, 128).transpose(0, 3, 2, 1)
            .reshape(12, 128, 4096)),
        "bqc": np.ascontiguousarray(bq_c.reshape(12, 128).T),
        "WqupT": np.ascontiguousarray(Wq_up[perm_pairs].T),
        "bqup": np.ascontiguousarray(bq_up[perm_pairs].reshape(48, 128).T),
        "WkvcT": np.ascontiguousarray(Wkv_c.T),
        "bkvc": np.ascontiguousarray(np.pad(bkv_c, (0, 64)).reshape(5, 128).T),
        "WkunT": np.ascontiguousarray(
            Wkv_up[perm_n].T.reshape(4, 128, 16, 256).transpose(2, 1, 0, 3)),
        "bkun": np.ascontiguousarray(bkv_up[perm_n].reshape(32, 128).T),
        "WkuvT": np.ascontiguousarray(
            Wkv_up[perm_v].T.reshape(4, 128, 16, 256).transpose(2, 1, 0, 3)),
        "bkuv": np.ascontiguousarray(bkv_up[perm_v].reshape(4, 1024)),
        "WoT": np.ascontiguousarray(Wo.T),
        "bo_r": np.ascontiguousarray(bo[None, :]),
        "WbT": np.ascontiguousarray(W_base.T),
        "bb_r": np.ascontiguousarray(b_base[None, :]),
        "encT": np.ascontiguousarray(
            np.concatenate([enc0, enc1], 0).T.reshape(32, 128, 40)
            .transpose(1, 0, 2).reshape(128, 1280)),
        "co0": np.ascontiguousarray(
            coefs0.reshape(3, 32, 128, 32).transpose(0, 2, 1, 3)
            .reshape(3, 128, 1024)),
        "co1": np.ascontiguousarray(
            coefs1.reshape(3, 32, 128, 8).transpose(0, 2, 1, 3)
            .reshape(3, 128, 256)),
        "mk0": np.ascontiguousarray(
            mask0.reshape(3, 32, 128, 32).transpose(0, 2, 1, 3)
            .reshape(3, 128, 1024)).view(np.uint8),
        "mk1": np.ascontiguousarray(
            mask1.reshape(3, 32, 128, 8).transpose(0, 2, 1, 3)
            .reshape(3, 128, 256)).view(np.uint8),
        "FTb": _build_FT(H),
        "identf": np.eye(128, dtype=f),
    }

    in_maps = []
    for core in range(8):
        b_i, q_i = core // 4, core % 4
        xr = np.ascontiguousarray(np.roll(x[b_i], -TM * q_i, axis=0))
        m = dict(common)
        m["x_all"] = xr
        m["xTsp"] = np.ascontiguousarray(
            xr.reshape(32, 128, 32, 128).transpose(0, 3, 2, 1))
        in_maps.append(m)

    nc = build_nc()
    res = run_bass_kernel_spmd(nc, in_maps, core_ids=list(range(8)))

    out = np.empty((B, S, H), dtype=f)
    for core in range(8):
        b_i, q_i = core // 4, core % 4
        out[b_i, TM * q_i:TM * (q_i + 1), :] = res.results[core]["y"]
    return out
